# version 05
# baseline (speedup 1.0000x reference)
import math
import numpy as np
import concourse.bass as bass
import concourse.mybir as mybir
from concourse.bass_utils import run_bass_kernel_spmd

F32 = mybir.dt.float32
BF16 = mybir.dt.bfloat16
I32 = mybir.dt.int32
AF = mybir.ActivationFunctionType
ALU = mybir.AluOpType
AX = mybir.AxisListType

B, T, E = 4, 2048, 2048
TL, P = 1024, 128
NF = E // P
INTER = 8192
N_CTRL = 130156
NSAMP = INTER * E
DT = (N_CTRL - 1.0) / (NSAMP - 1.0)
NCH_CORE = (NSAMP // 128) // 8
LAMBDA_INIT = 0.8 - 0.6 * math.exp(-0.3 * 12)
NEG = -1e30
SCALE = 1.0 / math.sqrt(128.0)
TWO_PI = 2.0 * math.pi
MAGIC = 12582912.0
EPS = 1e-5
PIH = math.pi / 2.0
RGALL = [[0, 1, 2, 3, 4, 5, 6, 7]]

SM_LAM, SM_G, SM_BIN, SM_BOUT = 0, 4, 6, 22
SM_BGX, SM_BGA, SM_AR, SM_AI = 38, 46, 54, 62
SM_HM, SM_NLAM, SM_GP, SM_SP, SM_N8SP = 70, 71, 72, 74, 82
SM_EPS, SM_NEG1, SM_PIH, SM_TINY = 90, 91, 92, 93

_C_POLY = np.array([
    [0.0, -0.5, 1.0, -0.5],
    [1.0, 0.0, -2.5, 1.5],
    [0.0, 0.5, 2.0, -1.5],
    [0.0, 0.0, -0.5, 0.5],
], dtype=np.float64)
_BCOEF = np.array([0.5, -1.0, 0.0, 1.0, -0.5], np.float64)
_ACOEF = np.array([0.5, -2.0, 3.0, -2.0, 0.5], np.float64)


def _spline_const_tables():
    from math import comb
    pdt = np.arange(128, dtype=np.float64) * DT
    Cmat = np.zeros((128, 20))
    for j in range(4):
        for m in range(4):
            v = np.zeros(128)
            for nn in range(m, 4):
                v += _C_POLY[j, nn] * comb(nn, m) * pdt ** (nn - m)
            Cmat[:, j * 4 + m] = v
    return (np.ascontiguousarray(Cmat.T).astype(np.float32),
            np.stack([pdt, np.ones(128)], 0).astype(np.float32),
            np.repeat(_BCOEF[:, None], 128, 1).astype(np.float32),
            np.repeat(_ACOEF[:, None], 128, 1).astype(np.float32))


def _spline_s0(kind, core):
    if kind == "gu":
        cols = []
        for hb in (2 * core, 2 * core + 1):
            base = hb * P
            cols.extend(i * E + base for i in range(INTER))
    else:
        cols = []
        for ht in (2 * core, 2 * core + 1):
            for it in range(64):
                for hh in range(P):
                    cols.append((ht * P + hh) * INTER + it * P)
    return np.asarray(cols, dtype=np.int64)


def _spline_aux(kind, core, cps):
    s0 = _spline_s0(kind, core)
    u0 = s0.astype(np.float64) * DT
    i_n = np.floor(u0).astype(np.int64)
    phi = u0 - i_n
    out = {"phim": np.tile(np.stack([phi ** m for m in range(4)], 0),
                           (5, 1)).astype(np.float32),
           "u2": np.stack([np.ones_like(phi), phi], 0).astype(np.float32)}
    for name, cp in cps:
        wj = np.stack([cp[np.clip(i_n - 1 + j, 0, N_CTRL - 1)]
                       for j in range(5)], 0).astype(np.float32)
        out["cpw5" + name] = np.ascontiguousarray(wj)
        out["cpw20" + name] = np.ascontiguousarray(np.repeat(wj, 4, axis=0))
    return out


class Buf:
    __slots__ = ("name", "writer", "readers")

    def __init__(self, name):
        self.name = name
        self.writer = None
        self.readers = []


class Sched:
    ENGINES = ("pe", "act", "dve", "pool", "sp")

    def __init__(self):
        self.ops = {e: [] for e in self.ENGINES}
        self.cnt = {"s_" + e: 0 for e in self.ENGINES}
        self.waited = {e: {} for e in self.ENGINES}
        self.pend = {e: [] for e in self.ENGINES}
        self.cc_id = 0

    def _dep(self, eng, deps, d, tag=""):
        if d is None:
            return
        sem, val = d
        if val <= self.waited[eng].get(sem, 0):
            return
        if deps.get(sem, 0) < val:
            deps[sem] = val

    def op(self, eng, fn, reads=(), writes=(), dma=False, commit=True, tag="",
           chan=None):
        if dma and eng in ("sp", "act"):
            assert chan, ("sp/act DMAs need a channel (HWDGE OOO)", tag)
        sem = ("c_" + chan) if (dma and chan) else ("s_" + eng)
        self.cnt.setdefault(sem, 0)
        deps = {}
        for b in reads:
            self._dep(eng, deps, b.writer, tag)
        for b in writes:
            self._dep(eng, deps, b.writer, tag)
            for d in b.readers:
                self._dep(eng, deps, d, tag)
        for s, v in deps.items():
            assert v <= self.cnt[s], (tag, s, v, self.cnt[s])
            if s.startswith("c_") and v < self.cnt[s]:
                print(f"WARNING partial chan wait {s} {v}<{self.cnt[s]} "
                      f"at {tag}")
            self.waited[eng][s] = max(self.waited[eng].get(s, 0), v)
        inc = (16 if dma else 1) if commit else None
        self.ops[eng].append((fn, deps, inc, sem, tag))
        if commit:
            self.cnt[sem] += inc
            here = (sem, self.cnt[sem])
            for b, isw in self.pend[eng]:
                if isw:
                    b.writer = here
                    b.readers = []
                else:
                    b.readers.append(here)
            self.pend[eng].clear()
            for b in writes:
                b.writer = here
                b.readers = []
            for b in reads:
                b.readers.append(here)
        else:
            for b in writes:
                self.pend[eng].append((b, True))
            for b in reads:
                self.pend[eng].append((b, False))

    def collective(self, fn, reads=(), writes=(), tag="cc"):
        name = f"s_cc{self.cc_id}"
        self.cc_id += 1
        self.cnt[name] = 1
        deps = {}
        for b in reads:
            self._dep("pool", deps, b.writer)
        for s, v in deps.items():
            self.waited["pool"][s] = max(self.waited["pool"].get(s, 0), v)
        self.ops["pool"].append((fn, deps, ("cc", name), None, tag))
        for b in writes:
            b.writer = (name, 1)
            b.readers = []

    def settle(self, bufs, chan):
        sem = "c_" + chan
        for b in bufs:
            b.writer = (sem, self.cnt[sem])

    def claim(self, newbuf, oldbufs):
        newbuf.writer = None
        rd = []
        for ob in oldbufs:
            if ob.writer:
                rd.append(ob.writer)
            rd.extend(ob.readers)
        newbuf.readers = rd

    def build(self, nc, es, block):
        sems = {}
        for name in sorted(self.cnt):
            assert self.cnt[name] < 60000, (name, self.cnt[name])
            sems[name] = es.enter_context(nc.semaphore(name))

        def mk_body(engname):
            oplist = self.ops[engname]

            def body(eng):
                for fn, deps, inc, opsem, tag in oplist:
                    for s in sorted(deps):
                        eng.wait_ge(sems[s], deps[s])
                    ins = fn(eng)
                    if inc is None:
                        continue
                    if isinstance(inc, tuple):
                        ins.then_inc(sems[inc[1]])
                    else:
                        ins.then_inc(sems[opsem], inc)
            return body

        for e, attr in (("sp", "sync"), ("pe", "tensor"), ("act", "scalar"),
                        ("dve", "vector"), ("pool", "gpsimd")):
            if self.ops[e]:
                getattr(block, attr)(mk_body(e))


def build_program(taps=()):
    import contextlib
    nc = bass.Bass("TRN2", target_bir_lowering=False)
    nc.disable_value_cache = True
    S = Sched()
    es = contextlib.ExitStack()
    taps = set(taps)

    DI = lambda n, s, d=F32: nc.dram_tensor(n, list(s), d, kind="ExternalInput")
    DO = lambda n, s, d=F32: nc.dram_tensor(n, list(s), d, kind="ExternalOutput")

    x_lo = DI("x_lo", (E, TL))
    x_hi = DI("x_hi", (E, TL))
    x_res = DI("x_res", (E, TL))
    wq_d = DI("wq", (E, 1024))
    wk_d = DI("wk", (E, 1024))
    wv_d = DI("wv", (E, 1024))
    wo_d = DI("wo", (1024, E))
    win_d = DI("win", (E, E))
    wout_d = DI("wout", (E, E))
    wgx_d = DI("wgx", (16 * P, P))
    wga_d = DI("wga", (16 * P, P))
    smalls_d = DI("smalls", (P, 96))
    offs_d = DI("offs", (1, 40), I32)
    diagm_d = DI("diagm", (P, P))
    ident_d = DI("ident", (P, P))
    cm_d = DI("cmatT", (20, P))
    ul_d = DI("u_lhs", (2, P))
    bl_d = DI("b_lhs", (5, P))
    al_d = DI("a_lhs", (5, P))
    gu_phim = DI("gu_phim", (20, NCH_CORE))
    gu_u2 = DI("gu_u2", (2, NCH_CORE))
    dn_phim = DI("dn_phim", (20, NCH_CORE))
    dn_u2 = DI("dn_u2", (2, NCH_CORE))
    cpw20 = {w: DI("cpw20_" + w, (20, NCH_CORE)) for w in ("g", "u", "d")}
    cpw5 = {w: DI("cpw5_" + w, (5, NCH_CORE)) for w in ("g", "u", "d")}

    out_res = DO("out_res", (E, TL))
    tap_outs = {}

    def tapout(name, shape, dt=F32):
        tap_outs[name] = DO("tap_" + name, shape, dt)
        return tap_outs[name]

    wshard = nc.dram_tensor("wshard", [768, INTER], BF16)
    wg = nc.dram_tensor("wg", [6144, INTER], BF16)
    qt_dram = nc.dram_tensor("qt_dram", [16, 8, P, P], BF16)
    pdram = nc.dram_tensor("pdram", [2 * TL, TL], BF16)
    xch_a = nc.dram_tensor("xch_a", [2 * E, T], F32, addr_space="Shared")
    xch_b = nc.dram_tensor("xch_b", [2 * TL, 2], F32, addr_space="Shared")
    parta = nc.dram_tensor("parta", [E, T], F32)
    partb = nc.dram_tensor("partb", [2 * E, TL], F32)
    partc = nc.dram_tensor("partc", [TL, 2], F32)
    partd = nc.dram_tensor("partd", [TL, 2], F32)
    bar_in = nc.dram_tensor("bar_in", [1, 8], F32)
    bar_out = [nc.dram_tensor(f"bar_out{i}", [8, 8], F32) for i in range(3)]

    res_t = es.enter_context(nc.sbuf_tensor("res", [P, NF * TL], F32))
    arena = es.enter_context(nc.sbuf_tensor("arena", [P, 61440], BF16))
    onesc = es.enter_context(nc.sbuf_tensor("onesc", [P, 1], F32))
    onescb = es.enter_context(nc.sbuf_tensor("onescb", [P, 1], BF16))
    onesr = es.enter_context(nc.sbuf_tensor("onesr", [1, P], F32))
    ones256 = es.enter_context(nc.sbuf_tensor("ones256", [P, 256], F32))
    identb = es.enter_context(nc.sbuf_tensor("ident_s", [P, P], BF16))
    diagm = es.enter_context(nc.sbuf_tensor("diagm_s", [P, P], F32))
    smalls = es.enter_context(nc.sbuf_tensor("smalls_s", [P, 96], F32))
    rows = es.enter_context(nc.sbuf_tensor("rows", [1, 1536], F32))
    dmp = es.enter_context(nc.sbuf_tensor("dmp", [1, 16], F32))
    dbgs = es.enter_context(nc.sbuf_tensor("dbgs", [P, 2048], F32))
    cols = es.enter_context(nc.sbuf_tensor("cols", [P, 16], F32))
    carry = es.enter_context(nc.sbuf_tensor("carry", [P, 32], F32))
    hend = es.enter_context(nc.sbuf_tensor("hend", [P, 16], F32))
    hin = es.enter_context(nc.sbuf_tensor("hin", [P, 16], F32))
    offs_s = es.enter_context(nc.sbuf_tensor("offs_s", [1, 40], I32))
    cm_t = es.enter_context(nc.sbuf_tensor("cm_t", [20, P], F32))
    ul_t = es.enter_context(nc.sbuf_tensor("ul_t", [2, P], F32))
    bl_t = es.enter_context(nc.sbuf_tensor("bl_t", [5, P], F32))
    al_t = es.enter_context(nc.sbuf_tensor("al_t", [5, P], F32))

    ps1 = es.enter_context(nc.psum_tensor("ps1", [P, 2048], F32))
    ps2 = es.enter_context(nc.psum_tensor("ps2", [P, 1024], F32))
    ps3 = es.enter_context(nc.psum_tensor("ps3", [P, 512], F32))
    psT = es.enter_context(nc.psum_tensor("psT", [P, 1024], BF16))

    b_res = Buf("res")
    b_ps1 = [Buf(f"ps1_{i}") for i in range(4)]
    b_ps2 = [Buf(f"ps2_{i}") for i in range(2)]
    b_ps3 = Buf("ps3")
    b_psT = Buf("psT")
    b_rows, b_cols, b_carry = Buf("rows"), Buf("cols"), Buf("carry")
    b_hend, b_hin = Buf("hend"), Buf("hin")
    b_smalls, b_offs = Buf("smalls"), Buf("offs")
    b_ones, b_ident, b_diagm, b_rct = (Buf("ones"), Buf("ident"),
                                       Buf("diagm"), Buf("rct"))
    b_wshard, b_wg, b_pdram = Buf("wshard"), Buf("wg"), Buf("pdram")
    b_qtd = Buf("qt_dram")
    b_xch_a, b_xch_b = Buf("xch_a"), Buf("xch_b")
    b_parta, b_partb = Buf("parta"), Buf("partb")
    b_partc, b_partd = Buf("partc"), Buf("partd")

    hold = {}

    def dma(q, out_ap, in_ap, reads=(), writes=(), tag="", chan=None):
        S.op(q, lambda e, o=out_ap, i=in_ap: e.dma_start(out=o, in_=i),
             reads=reads, writes=writes, dma=True, tag=tag, chan=chan)

    def mm(out_ap, lhsT, rhs, start, stop, reads=(), writes=(), tag="",
           commit=None):
        S.op("pe",
             lambda e, o=out_ap, l=lhsT, r=rhs, a=start, b=stop:
             e.matmul(o, l, r, start=a, stop=b),
             reads=reads, writes=writes,
             commit=(stop if commit is None else commit), tag=tag)

    def trans(out_ap, in_ap, reads=(), writes=(), tag=""):
        S.op("pe", lambda e, o=out_ap, i=in_ap: e.transpose(o, i, identb[:, :]),
             reads=list(reads) + [b_ident], writes=writes, tag=tag)

    def act(out_ap, in_ap, func, bias=0.0, scale=1.0, accum=None,
            reads=(), writes=(), tag=""):
        S.op("act",
             lambda e, o=out_ap, i=in_ap, f=func, bi=bias, sc=scale, ac=accum:
             e.activation(o, i, f, bias=bi, scale=sc, accum_out=ac),
             reads=reads, writes=writes, tag=tag)

    def dve(meth, *aps, reads=(), writes=(), tag="", eng="dve"):
        S.op("dve", lambda e, m=meth, a=aps: getattr(e, m)(*a),
             reads=reads, writes=writes, tag=tag)

    def abf(off_kb, cols_):
        o = int(off_kb * 512)
        return arena[:, o:o + cols_]

    def af32(off_kb, cols_):
        o = int(off_kb * 512)
        return arena[:, o:o + 2 * cols_].bitcast(F32)

    dve("memset", onesc[:, :], 1.0, writes=[b_ones], tag="o1")
    dve("memset", onescb[:, :], 1.0, writes=[b_ones], tag="o1b")
    dve("memset", onesr[:, :], 1.0, writes=[b_ones], tag="o2")
    dve("memset", ones256[:, :], 1.0, writes=[b_ones], tag="o3")
    dve("memset", carry[:, :], 0.0, writes=[b_carry], tag="o4")
    dma("pool", identb[:, :], ident_d[:, :], writes=[b_ident], tag="lid",
        chan="lid")
    dma("sp", diagm[:, :], diagm_d[:, :], writes=[b_diagm], tag="ldg",
        chan="ldg")
    dma("act", smalls[:, :], smalls_d[:, :], writes=[b_smalls], tag="lsm",
        chan="sml")
    dma("sp", offs_s[:, :], offs_d[:, :], writes=[b_offs], tag="lof",
        chan="lof")
    for t_, d_ in ((cm_t, cm_d), (ul_t, ul_d), (bl_t, bl_d), (al_t, al_d)):
        dma("sp", t_[:, :], d_[:, :], writes=[b_rct], tag="lrc", chan="lrc")

    def dyn_dma(idx, mx, out_fn, in_ap, reads=(), writes=(), tag=""):
        def f(e, idx=idx, mx=mx, out_fn=out_fn, in_ap=in_ap):
            r = e.alloc_register(None)
            e.reg_load(r, offs_s[0:1, idx:idx + 1])
            off = e.snap(r, min_val=0, max_val=mx)
            ins = e.dma_start(out=out_fn(off), in_=in_ap)
            e.free_register(r)
            return ins
        S.op("pool", f, reads=list(reads) + [b_offs], writes=writes, dma=True,
             tag=tag)

    def dyn_dma_in(idx, mx, out_ap, in_fn, reads=(), writes=(), tag=""):
        def f(e, idx=idx, mx=mx, out_ap=out_ap, in_fn=in_fn):
            r = e.alloc_register(None)
            e.reg_load(r, offs_s[0:1, idx:idx + 1])
            off = e.snap(r, min_val=0, max_val=mx)
            ins = e.dma_start(out=out_ap, in_=in_fn(off))
            e.free_register(r)
            return ins
        S.op("pool", f, reads=list(reads) + [b_offs], writes=writes, dma=True,
             tag=tag)

    for f in range(NF):
        dma("sp", res_t[:, f * TL:(f + 1) * TL], x_res[f * P:(f + 1) * P, :],
            writes=[b_res], tag="lre", chan="lre")

    dve("tensor_tensor", cols[:, 0:1], smalls[:, SM_LAM:SM_LAM + 1],
        smalls[:, SM_LAM + 1:SM_LAM + 2], ALU.mult,
        reads=[b_smalls], writes=[b_cols], tag="lm1")
    dve("tensor_tensor", cols[:, 1:2], smalls[:, SM_LAM + 2:SM_LAM + 3],
        smalls[:, SM_LAM + 3:SM_LAM + 4], ALU.mult,
        reads=[b_smalls], writes=[b_cols], tag="lm2")
    mm(ps3[0:1, 0:2], onesc[:, :], cols[:, 0:2], True, True,
       reads=[b_cols, b_ones], writes=[b_ps3], tag="lmd")
    act(rows[0:1, 0:2], ps3[0:1, 0:2], AF.Exp, reads=[b_ps3], writes=[b_rows],
        tag="lme")
    dve("tensor_tensor", rows[0:1, 2:3], rows[0:1, 0:1], rows[0:1, 1:2],
        ALU.subtract, reads=[b_rows], writes=[b_rows], tag="lms")
    dve("tensor_scalar", rows[0:1, 3:4], rows[0:1, 2:3], -1.0, -LAMBDA_INIT,
        ALU.mult, ALU.add, reads=[b_rows], writes=[b_rows], tag="lmn")
    b_dmp = Buf("dmp")
    b_dbgs = Buf("dbgs")
    if "dbg" in taps:
        dve("tensor_copy", dmp[0:1, 6:10], smalls[0:1, 0:4],
            reads=[b_smalls], writes=[b_dmp], tag="dcp0")
        dve("tensor_copy", dmp[0:1, 10:12], cols[0:1, 0:2],
            reads=[b_cols], writes=[b_dmp], tag="dcp0b")
        dve("tensor_copy", dmp[0:1, 0:2], ps3[0:1, 0:2],
            reads=[b_ps3], writes=[b_dmp], tag="dcp1")
    mm(ps3[:, 0:1], onesr[:, :], rows[0:1, 3:4], True, True,
       reads=[b_rows, b_ones], writes=[b_ps3], tag="lmb")
    dve("tensor_copy", smalls[:, SM_NLAM:SM_NLAM + 1], ps3[:, 0:1],
        reads=[b_ps3], writes=[b_smalls], tag="lmc")
    if "dbg" in taps:
        dve("tensor_copy", dmp[0:1, 2:6], rows[0:1, 0:4],
            reads=[b_rows], writes=[b_dmp], tag="dcp2")
    dve("tensor_scalar", smalls[:, SM_GP:SM_GP + 2], smalls[:, SM_G:SM_G + 2],
        1.0 - LAMBDA_INIT, None, ALU.mult,
        reads=[b_smalls], writes=[b_smalls], tag="gp")
    dve("tensor_scalar", cols[:, 0:8], smalls[:, SM_AR:SM_AR + 8], -1.0, None,
        ALU.mult, reads=[b_smalls], writes=[b_cols], tag="sp1")
    dve("tensor_tensor", cols[:, 8:16], cols[:, 0:8],
        smalls[:, SM_AR:SM_AR + 8], ALU.min,
        reads=[b_cols, b_smalls], writes=[b_cols], tag="sp2")
    act(cols[:, 8:16], cols[:, 8:16], AF.Exp, reads=[b_cols], writes=[b_cols],
        tag="sp3")
    act(cols[:, 8:16], cols[:, 8:16], AF.Ln, bias=1.0,
        reads=[b_cols], writes=[b_cols], tag="sp4")
    dve("tensor_scalar_max", cols[:, 0:8], smalls[:, SM_AR:SM_AR + 8], 0.0,
        reads=[b_smalls], writes=[b_cols], tag="sp5")
    dve("tensor_tensor", smalls[:, SM_SP:SM_SP + 8], cols[:, 0:8],
        cols[:, 8:16], ALU.add, reads=[b_cols], writes=[b_smalls], tag="sp6")
    dve("tensor_scalar", smalls[:, SM_N8SP:SM_N8SP + 8],
        smalls[:, SM_SP:SM_SP + 8], -8.0, None, ALU.mult,
        reads=[b_smalls], writes=[b_smalls], tag="sp7")

    def emit_ln_chunk(dst_ap_fn, dst_bufs, tag, sq_off, scr_parents):
        sq = abf(sq_off, 512)
        bc1 = abf(sq_off + 1, 512)
        bc2 = abf(sq_off + 2, 512)
        b_sq = Buf(tag + "sq")
        b_bc = Buf(tag + "bc")
        S.claim(b_sq, scr_parents)
        S.claim(b_bc, scr_parents)
        for f in range(NF):
            act(sq[:, :], dst_ap_fn(f), AF.Square,
                reads=[dst_bufs[f]], writes=[b_sq], tag=tag + "s")
            mm(ps3[0:1, 0:512], onescb[:, :], dst_ap_fn(f),
               start=(f == 0), stop=(f == NF - 1),
               reads=[dst_bufs[f], b_ones], writes=[b_ps3], tag=tag + "m1")
            mm(ps2[0:1, 0:512], onescb[:, :], sq[:, :],
               start=(f == 0), stop=(f == NF - 1),
               reads=[b_sq, b_ones], writes=[b_ps2[0]], tag=tag + "m2")
        dve("tensor_scalar", rows[0:1, 0:512], ps3[0:1, 0:512], 1.0 / E, None,
            ALU.mult, reads=[b_ps3], writes=[b_rows], tag=tag + "mn")
        dve("tensor_tensor", rows[0:1, 512:1024], rows[0:1, 0:512],
            rows[0:1, 0:512], ALU.mult, reads=[b_rows], writes=[b_rows],
            tag=tag + "m2r")
        dve("scalar_tensor_tensor", rows[0:1, 512:1024], ps2[0:1, 0:512],
            1.0 / E, rows[0:1, 512:1024], ALU.mult, ALU.subtract,
            reads=[b_ps2[0], b_rows], writes=[b_rows], tag=tag + "v")
        act(rows[0:1, 512:1024], rows[0:1, 512:1024], AF.Ln,
            bias=smalls[0:1, SM_EPS:SM_EPS + 1],
            reads=[b_rows], writes=[b_rows], tag=tag + "l")
        act(rows[0:1, 512:1024], rows[0:1, 512:1024], AF.Exp, scale=-0.5,
            reads=[b_rows], writes=[b_rows], tag=tag + "e")
        dve("tensor_tensor", rows[0:1, 1024:1536], rows[0:1, 0:512],
            rows[0:1, 512:1024], ALU.mult, reads=[b_rows], writes=[b_rows],
            tag=tag + "mr")
        mm(ps3[:, 0:512], onesr[:, :], rows[0:1, 512:1024], True, True,
           reads=[b_rows, b_ones], writes=[b_ps3], tag=tag + "b1")
        mm(ps2[:, 0:512], onesr[:, :], rows[0:1, 1024:1536], True, True,
           reads=[b_rows, b_ones], writes=[b_ps2[0]], tag=tag + "b2")
        dve("tensor_copy", bc1[:, :], ps3[:, 0:512], reads=[b_ps3],
            writes=[b_bc], tag=tag + "x1")
        dve("tensor_copy", bc2[:, :], ps2[:, 0:512], reads=[b_ps2[0]],
            writes=[b_bc], tag=tag + "x2")
        for f in range(NF):
            eng = "dve" if f % 2 == 0 else "pool"
            dve("tensor_tensor", dst_ap_fn(f), dst_ap_fn(f), bc1[:, :],
                ALU.mult, reads=[b_bc], writes=[dst_bufs[f]], tag=tag + "a1",
                eng=eng)
            dve("tensor_tensor", dst_ap_fn(f), dst_ap_fn(f), bc2[:, :],
                ALU.subtract, reads=[b_bc], writes=[dst_bufs[f]],
                tag=tag + "a2", eng=eng)
        return [b_sq, b_bc]

    AK_XNC, AK_WS, AK_KT, AK_V, AK_QST = 0, 16, 24, 56, 88
    xnc_ap = lambda f: abf(AK_XNC + f, 512)
    ws_ap = abf(AK_WS, 4096)
    kt_ap = lambda i: abf(AK_KT + 4 * i, 2048)
    v_ap = lambda tt: abf(AK_V + 2 * tt, 1024)
    qst_ap = abf(AK_QST, 512)

    b_kt, b_v = Buf("kt"), Buf("v")
    b_ws = Buf("ws")
    b_qst = Buf("qst")
    stag = [af32(92, 2048), af32(100, 2048)]
    stagv = af32(92, 4096)
    ws2_ap = abf(108, 4096)
    ws_slots = [ws_ap, ws2_ap]
    b_stag = [Buf("stag0"), Buf("stag1")]
    b_wsl2 = [b_ws, Buf("ws2")]
    wstg_ctr = [0]
    xstag = [af32(116, 512), af32(118, 512)]
    b_xstag = [Buf("xstag0"), Buf("xstag1")]
    prev_xt = None
    ln_scr = [Buf("z0")]
    for tch in range(4):
        xsrc = x_lo if tch < 2 else x_hi
        c0 = (tch % 2) * 512
        b_xt = [Buf(f"xt{tch}_{f}") for f in range(NF)]
        if prev_xt is not None:
            for f in range(NF):
                S.claim(b_xt[f], [prev_xt[f]])
        prev_xt = b_xt
        for f in range(NF):
            xp = f % 2
            b_xs = Buf(f"xs{tch}_{f}")
            S.claim(b_xs, [b_xstag[xp]])
            b_xstag[xp] = b_xs
            dma("sp" if xp == 0 else "act", xstag[xp][:, :],
                xsrc[f * P:(f + 1) * P, c0:c0 + 512],
                writes=[b_xs], tag="lx", chan=f"lx{xp}")
            dve("tensor_copy", xnc_ap(f), xstag[xp][:, :],
                reads=[b_xs], writes=[b_xt[f]], tag="xcast")
        ln_scr = emit_ln_chunk(xnc_ap, b_xt, f"l{tch}", AK_QST + 1, ln_scr)
        if "xn" in taps:
            if "xn" not in tap_outs:
                tapout("xn", (E, T))
            for f in range(NF):
                dma("pool", tap_outs["xn"][f * P:(f + 1) * P,
                                           tch * 512:(tch + 1) * 512],
                    xnc_ap(f), reads=[b_xt[f]], tag="txn")
        for wdram, wtag in ((wq_d, "q"), (wk_d, "k")):
            for qc in range(8):
                p = wstg_ctr[0] % 2
                wstg_ctr[0] += 1
                b_sn = Buf(f"sg{wtag}{tch}{qc}")
                S.claim(b_sn, [b_stag[p]])
                b_stag[p] = b_sn
                for kk in range(4):
                    dma("sp" if p == 0 else "act",
                        stag[p][:, kk * 512:(kk + 1) * 512].rearrange(
                            "p (k c) -> p k c", k=4),
                        wdram[kk * 512:(kk + 1) * 512,
                              qc * P:(qc + 1) * P].rearrange(
                                  "(k p) c -> p k c", p=P),
                        writes=[b_sn], tag=f"lw{wtag}", chan=f"wq{p}")
                b_wn = Buf(f"w{wtag}{tch}{qc}")
                S.claim(b_wn, [b_wsl2[p]])
                b_wsl2[p] = b_wn
                dve("tensor_copy", ws_slots[p][:, 0:2048], stag[p][:, :],
                    reads=[b_sn], writes=[b_wn], tag="wcast")
                pslice = ps1[:, (qc % 2) * 512:(qc % 2) * 512 + 512]
                pbuf = b_ps1[qc % 2]
                for ek in range(16):
                    mm(pslice, ws_slots[p][:, ek * P:(ek + 1) * P],
                       xnc_ap(ek), start=(ek == 0), stop=(ek == 15),
                       reads=[b_wn, b_xt[ek]], writes=[pbuf], tag=f"p{wtag}")
                if wtag == "k":
                    act(kt_ap(qc)[:, tch * 512:tch * 512 + 512], pslice,
                        AF.Copy, reads=[pbuf], writes=[b_kt], tag="ek")
                else:
                    b_qn = Buf(f"qs{tch}{qc}")
                    S.claim(b_qn, [b_qst])
                    b_qst = b_qn
                    act(qst_ap[:, :], pslice, AF.Copy,
                        reads=[pbuf], writes=[b_qst], tag="eq")
                    for qbi in range(4):
                        qb = tch * 4 + qbi
                        dma("sp", qt_dram[qb, qc, :, :],
                            qst_ap[:, qbi * P:(qbi + 1) * P],
                            reads=[b_qst], writes=[b_qtd], tag="sq",
                            chan="sq")
        for ec2 in range(4):
            b_sn = Buf(f"sv{tch}{ec2}")
            S.claim(b_sn, [b_stag[0], b_stag[1]])
            b_stag[0] = b_stag[1] = b_sn
            for kk in range(4):
                dma("sp" if kk % 2 == 0 else "act",
                    stagv[:, kk * 1024:kk * 1024 + 1024].rearrange(
                        "p (k c) -> p k c", k=4),
                    wv_d[kk * 512:(kk + 1) * 512,
                         ec2 * 256:(ec2 + 1) * 256].rearrange(
                             "(k p) c -> p k c", p=P),
                    writes=[b_sn], tag="lwv", chan="wv")
            pv = ec2 % 2
            b_wn = Buf(f"wv{tch}{ec2}")
            S.claim(b_wn, [b_wsl2[pv]])
            b_wsl2[pv] = b_wn
            dve("tensor_copy", ws_slots[pv][:, 0:4096], stagv[:, :],
                reads=[b_sn], writes=[b_wn], tag="vcast")
            for ts4 in range(4):
                tt = tch * 4 + ts4
                pslice = ps2[:, (ts4 % 2) * 512:(ts4 % 2) * 512 + 256]
                pbuf = b_ps2[ts4 % 2]
                for ek in range(16):
                    mm(pslice, xnc_ap(ek)[:, ts4 * P:(ts4 + 1) * P],
                       ws_slots[pv][:, ek * 256:(ek + 1) * 256],
                       start=(ek == 0), stop=(ek == 15),
                       reads=[b_wn, b_xt[ek]], writes=[pbuf], tag="pv")
                act(v_ap(tt)[:, ec2 * 256:(ec2 + 1) * 256], pslice, AF.Copy,
                    reads=[pbuf], writes=[b_v], tag="ev")

    qkv_scrap = [b_stag[0], b_stag[1], b_wsl2[0], b_wsl2[1]]
    b_ws = b_wsl2[0]
    if "kt" in taps:
        tp = tapout("kt", (1024, T))
        for i in range(8):
            dma("pool", tp[i * P:(i + 1) * P, :], kt_ap(i), reads=[b_kt],
                tag="tkt")
    if "vv" in taps:
        tp = tapout("vv", (T, 1024))
        for tt in range(16):
            dma("pool", tp[tt * P:(tt + 1) * P, :], v_ap(tt), reads=[b_v],
                tag="tvv")
    if "qt" in taps:
        tp = tapout("qt", (16 * 8 * P, P), BF16)
        dma("pool", tp[:, :], qt_dram[:, :, :, :].rearrange(
            "a b p q -> (a b p) q"), reads=[b_qtd], tag="tqt")

    R0 = 92

    def emit_recon():
        r20a = [af32(R0 + 2 * i, 512)[0:20, :] for i in range(2)]
        rpha = [af32(R0 + 4 + 2 * i, 512)[0:20, :] for i in range(2)]
        ru2a = [af32(R0 + 8 + 2 * i, 512)[0:2, :] for i in range(2)]
        rc5a = [af32(R0 + 12 + 2 * i, 512)[0:5, :] for i in range(2)]
        rxa = af32(R0 + 16, 512)[0:20, :]
        rva = af32(R0 + 18, 1024)
        raba = af32(R0 + 22, 1024)
        rwa = abf(R0 + 26, 1024)
        b_r20 = [Buf(f"r20{i}") for i in range(2)]
        b_rph = [Buf(f"rph{i}") for i in range(2)]
        b_ru2 = [Buf(f"ru2{i}") for i in range(2)]
        b_rc5 = [Buf(f"rc5{i}") for i in range(2)]
        b_rx, b_rv, b_rab, b_rw = Buf("rx"), Buf("rv"), Buf("rab"), Buf("rw")
        for _rb in (b_r20 + b_rph + b_ru2 + b_rc5 + [b_rx, b_rv, b_rab, b_rw]):
            S.claim(_rb, qkv_scrap)
        wrows = {"g": 0, "u": 256, "d": 512}
        for wname in ("g", "u", "d"):
            phim_ = gu_phim if wname in ("g", "u") else dn_phim
            u2_ = gu_u2 if wname in ("g", "u") else dn_u2
            for ch in range(NCH_CORE // 512):
                sl = ch % 2
                c0 = ch * 512
                dma("sp", r20a[sl][:, :], cpw20[wname][:, c0:c0 + 512],
                    writes=[b_r20[sl]], tag="r20", chan=f"r2{sl}")
                dma("sp", rpha[sl][:, :], phim_[:, c0:c0 + 512],
                    writes=[b_rph[sl]], tag="rph", chan=f"rp{sl}")
                dma("act", ru2a[sl][:, :], u2_[:, c0:c0 + 512],
                    writes=[b_ru2[sl]], tag="ru2", chan=f"rq{sl}")
                dma("act", rc5a[sl][:, :], cpw5[wname][:, c0:c0 + 512],
                    writes=[b_rc5[sl]], tag="rc5", chan=f"rf{sl}")
                dve("tensor_tensor", rxa[:, :], r20a[sl][:, :],
                    rpha[sl][:, :], ALU.mult,
                    reads=[b_r20[sl], b_rph[sl]], writes=[b_rx], tag="rx")
                mm(ps1[:, 1024:1536], cm_t[:, :], rxa[:, :], True, True,
                   reads=[b_rx, b_rct], writes=[b_ps1[2]], tag="rwa")
                mm(ps1[:, 1536:2048], ul_t[:, :], ru2a[sl][:, :], True, True,
                   reads=[b_ru2[sl], b_rct], writes=[b_ps1[3]], tag="ru")
                mm(ps2[:, 512:1024], bl_t[:, :], rc5a[sl][:, :], True, True,
                   reads=[b_rc5[sl], b_rct], writes=[b_ps2[1]], tag="rbb")
                mm(psT[:, 0:1024].bitcast(F32)[:, 0:256], al_t[:, :],
                   rc5a[sl][:, 0:256], True, True,
                   reads=[b_rc5[sl], b_rct], writes=[b_psT], tag="ra1")
                mm(psT[:, 0:1024].bitcast(F32)[:, 256:512], al_t[:, :],
                   rc5a[sl][:, 256:512], True, True,
                   reads=[b_rc5[sl], b_rct], writes=[b_psT], tag="ra2")
                act(rva[:, 0:512], ps1[:, 1536:2048], AF.Relu,
                    bias=smalls[:, SM_NEG1:SM_NEG1 + 1],
                    reads=[b_ps1[3], b_smalls], writes=[b_rv], tag="rv")
                act(rva[:, 512:1024], rva[:, 0:512], AF.Square,
                    reads=[b_rv], writes=[b_rv], tag="rv2")
                act(rwa[:, 0:512], ps1[:, 1024:1536], AF.Copy,
                    reads=[b_ps1[2]], writes=[b_rw], tag="rwe")
                dve("tensor_tensor", raba[:, 0:512], rva[:, 0:512],
                    psT[:, 0:1024].bitcast(F32)[:, 0:512], ALU.mult,
                    reads=[b_rv, b_psT], writes=[b_rab], tag="rm1")
                dve("tensor_tensor", raba[:, 512:1024], raba[:, 0:512],
                    ps2[:, 512:1024], ALU.add, reads=[b_rab, b_ps2[1]],
                    writes=[b_rab], tag="rm2")
                dve("tensor_tensor", raba[:, 0:512], rva[:, 512:1024],
                    raba[:, 512:1024], ALU.mult, reads=[b_rv, b_rab],
                    writes=[b_rab], tag="rm3", eng="pool")
                dve("tensor_tensor", rwa[:, 512:1024], raba[:, 0:512],
                    rwa[:, 0:512], ALU.add, reads=[b_rab, b_rw],
                    writes=[b_rw], tag="rw", eng="pool")
                if wname in ("g", "u"):
                    hb_l, i0 = divmod(c0, INTER)
                    dst = wshard[wrows[wname] + hb_l * P:
                                 wrows[wname] + hb_l * P + P, i0:i0 + 512]
                else:
                    ht_l, i0 = divmod(c0, INTER)
                    dst = wshard[512 + ht_l * P:512 + ht_l * P + P,
                                 i0:i0 + 512]
                dma("sp", dst, rwa[:, 512:1024], reads=[b_rw],
                    writes=[b_wshard], tag="rwo", chan="rwo")

        def cc_w(e):
            return e.collective_compute(
                "AllGather", ALU.bypass, replica_groups=RGALL,
                ins=[wshard.ap().opt()], outs=[wg.ap().opt()])
        S.collective(cc_w, reads=[b_wshard], writes=[b_wg], tag="agw")
        return ([b_r20[0], b_r20[1], b_rph[0], b_rph[1], b_ru2[0], b_ru2[1],
                 b_rc5[0], b_rc5[1], b_rx, b_rv, b_rab, b_rw])
    recon_bufs = emit_recon()
    if "wshard" in taps:
        tp = tapout("wshard", (768, INTER), BF16)
        tb = Buf("tws")
        sl = abf(R0, 2048)
        S.claim(tb, recon_bufs)
        for r in range(0, 768, P):
            for cc0 in range(0, INTER, 2048):
                dma("pool", sl[:, 0:2048], wshard[r:r + P, cc0:cc0 + 2048],
                    reads=[b_wshard], writes=[tb], tag="tw1")
                dma("pool", tp[r:r + P, cc0:cc0 + 2048], sl[:, 0:2048],
                    reads=[tb], tag="tw2")
        recon_bufs = recon_bufs + [tb]

    if "dbg" in taps:
        tapout("dbg_lam", (P, 20))
        dma("pool", tap_outs["dbg_lam"][:, 0:4], smalls[:, 70:74],
            reads=[b_smalls], tag="tlam")
        dma("pool", tap_outs["dbg_lam"][0:1, 4:16], dmp[0:1, 0:12],
            reads=[b_dmp], tag="tlam2")
        tapout("dbg_z", (P, 8))
        tapout("dbg_e", (P, 768), BF16)
        tapout("dbg_at", (P, 768), BF16)
        tapout("dbg_og", (8 * P, 512), BF16)

    def emit_attn():
        wos_ap = abf(0, 4096)
        wos_slots = [Buf("wos0"), Buf("wos1")]
        S.claim(wos_slots[0], prev_xt + [b_ws] + ln_scr)
        S.claim(wos_slots[1], prev_xt + [b_ws] + ln_scr)
        wostag = [af32(8, 1024), af32(12, 1024)]
        b_wostag = [Buf("wostag0"), Buf("wostag1")]
        S.claim(b_wostag[0], prev_xt + [b_ws] + ln_scr)
        S.claim(b_wostag[1], prev_xt + [b_ws] + ln_scr)
        qtb_ap = abf(88, 1024)
        b_qtb = Buf("qtb")
        S.claim(b_qtb, [b_qst])
        e_ap = lambda cc: abf(90 + 4 * cc, 2048)
        b_e = Buf("e12")
        S.claim(b_e, recon_bufs)
        at_ap = abf(98, 2048)
        b_at = Buf("at")
        S.claim(b_at, recon_bufs)
        og_ap = lambda ok: abf(102 + ok, 512)
        b_og = Buf("og")
        S.claim(b_og, recon_bufs)
        osq_ap = abf(110, 256)
        rb_ap = abf(110.5, 128)
        b_osq, b_rb = Buf("osq"), Buf("rb")
        S.claim(b_osq, recon_bufs)
        S.claim(b_rb, recon_bufs)
        stg_ap = af32(112, 1024)
        b_stg = Buf("stg")
        S.claim(b_stg, recon_bufs)
        wop_ps = [ps1[:, 0:512], ps1[:, 512:1024], ps1[:, 1024:1536],
                  ps1[:, 1536:2048], ps2[:, 0:512], ps2[:, 512:1024],
                  ps3[:, 0:512], psT[:, 0:1024].bitcast(F32)[:, 0:512]]
        wop_pb = [b_ps1[0], b_ps1[1], b_ps1[2], b_ps1[3], b_ps2[0], b_ps2[1],
                  b_ps3, b_psT]

        for qb in range(16):
            dma("sp", qtb_ap[:, :].rearrange("p (b q) -> p b q", b=8),
                qt_dram[qb, :, :, :].rearrange("b p q -> p b q"),
                reads=[b_qtd], writes=[b_qtb], tag="lqt", chan="lqt")
            for h in range(4):
                ext = (qb + 1) * P
                nkc = (ext + 511) // 512
                for cc in range(2):
                    qci = h * 2 + cc
                    for kc in range(nkc):
                        w = min(512, ext - kc * 512)
                        mm(ps1[:, kc * 512:kc * 512 + w],
                           qtb_ap[:, qci * P:(qci + 1) * P],
                           kt_ap(qci)[:, kc * 512:kc * 512 + w],
                           start=True, stop=True,
                           reads=[b_qtb, b_kt], writes=[b_ps1[kc]], tag="sc")
                    dpb = b_ps1[(ext - P) // 512]
                    dve("tensor_tensor", ps1[:, ext - P:ext],
                        ps1[:, ext - P:ext], diagm[:, :], ALU.add,
                        reads=[b_diagm, dpb], writes=[dpb], tag="msk")
                    rmx = cols[:, 8 + cc:9 + cc]
                    dve("reduce_max", rmx, ps1[:, 0:ext], AX.X,
                        reads=[b_ps1[kc] for kc in range(nkc)],
                        writes=[b_cols], tag="rmx")
                    dve("tensor_scalar", rmx, rmx, -SCALE, None, ALU.mult,
                        reads=[b_cols], writes=[b_cols], tag="rm2")
                    act(e_ap(cc)[:, 0:ext], ps1[:, 0:ext], AF.Exp,
                        bias=rmx, scale=SCALE, accum=cols[:, 10 + cc:11 + cc],
                        reads=[b_ps1[kc] for kc in range(nkc)] + [b_cols],
                        writes=[b_e, b_cols], tag="exp")
                if "dbg" in taps and qb == 5 and h == 1:
                    dma("pool", tap_outs["dbg_z"][:, :], cols[:, 8:16],
                        reads=[b_cols], tag="tz")
                dve("reciprocal", cols[:, 12:13], cols[:, 10:11],
                    reads=[b_cols], writes=[b_cols], tag="c1")
                dve("reciprocal", cols[:, 13:14], cols[:, 11:12],
                    reads=[b_cols], writes=[b_cols], tag="c2")
                dve("tensor_tensor", cols[:, 13:14], cols[:, 13:14],
                    smalls[:, SM_NLAM:SM_NLAM + 1], ALU.mult,
                    reads=[b_cols, b_smalls], writes=[b_cols], tag="c2n")
                dve("tensor_scalar", e_ap(0)[:, 0:ext], e_ap(0)[:, 0:ext],
                    cols[:, 12:13], None, ALU.mult,
                    reads=[b_cols], writes=[b_e], tag="an")
                dve("scalar_tensor_tensor", e_ap(0)[:, 0:ext],
                    e_ap(1)[:, 0:ext], cols[:, 13:14], e_ap(0)[:, 0:ext],
                    ALU.mult, ALU.add, reads=[b_cols], writes=[b_e], tag="ac")
                if "dbg" in taps and qb == 5 and h == 1:
                    dma("pool", tap_outs["dbg_e"][:, :], e_ap(0)[:, 0:768],
                        reads=[b_e], tag="te")
                nkb = qb + 1
                for kb0 in range(0, nkb, 8):
                    kb1 = min(kb0 + 8, nkb)
                    for kb in range(kb0, kb1):
                        sl = kb - kb0
                        trans(psT[:, sl * P:(sl + 1) * P],
                              e_ap(0)[:, kb * P:(kb + 1) * P],
                              reads=[b_e], writes=[b_psT], tag="tr")
                    n = (kb1 - kb0) * P
                    act(at_ap[:, kb0 * P:kb0 * P + n], psT[:, 0:n], AF.Copy,
                        reads=[b_psT], writes=[b_at], tag="ate")
                if "dbg" in taps and qb == 5 and h == 1:
                    dma("pool", tap_outs["dbg_at"][:, :], at_ap[:, 0:768],
                        reads=[b_at], tag="ta")
                for ee in range(2):
                    for kb in range(nkb):
                        mm(ps2[:, ee * 512:ee * 512 + P],
                           v_ap(kb)[:, h * 256 + ee * P:h * 256 + (ee + 1) * P],
                           at_ap[:, kb * P:(kb + 1) * P],
                           start=(kb == 0), stop=(kb == nkb - 1),
                           reads=[b_v, b_at], writes=[b_ps2[ee]], tag="av")
                for ee in range(2):
                    act(osq_ap[:, ee * P:(ee + 1) * P],
                        ps2[:, ee * 512:ee * 512 + P], AF.Square,
                        reads=[b_ps2[ee]], writes=[b_osq], tag="osq")
                for ee in range(2):
                    mm(ps3[0:1, 0:P], onescb[:, :],
                       osq_ap[:, ee * P:(ee + 1) * P],
                       start=(ee == 0), stop=(ee == 1),
                       reads=[b_osq, b_ones], writes=[b_ps3], tag="oms")
                act(rows[0:1, 0:P], ps3[0:1, 0:P], AF.Ln,
                    bias=smalls[0:1, SM_EPS:SM_EPS + 1], scale=1.0 / 256.0, reads=[b_ps3], writes=[b_rows],
                    tag="oln")
                act(rows[0:1, 0:P], rows[0:1, 0:P], AF.Exp, scale=-0.5,
                    reads=[b_rows], writes=[b_rows], tag="oex")
                mm(ps3[:, 256:256 + P], onesr[:, :], rows[0:1, 0:P], True,
                   True, reads=[b_rows, b_ones], writes=[b_ps3], tag="obc")
                dve("tensor_copy", rb_ap[:, :], ps3[:, 256:256 + P],
                    reads=[b_ps3], writes=[b_rb], tag="orb")
                for ee in range(2):
                    dve("scalar_tensor_tensor",
                        og_ap(h * 2 + ee)[:, (qb % 4) * P:(qb % 4 + 1) * P],
                        ps2[:, ee * 512:ee * 512 + P],
                        smalls[:, SM_GP + ee:SM_GP + ee + 1], rb_ap[:, :],
                        ALU.mult, ALU.mult,
                        reads=[b_ps2[ee], b_smalls, b_rb], writes=[b_og],
                        tag="on")
            if "dbg" in taps and qb == 7:
                for ok in range(8):
                    dma("pool", tap_outs["dbg_og"][ok * P:(ok + 1) * P, :],
                        og_ap(ok)[:, :], reads=[b_og], tag="to")
            if qb % 4 == 3:
                qbg = qb // 4
                for half in range(2):
                    for ok in range(8):
                        sl2 = ok % 2
                        b_wn = Buf(f"wo{qbg}{half}{ok}")
                        S.claim(b_wn, [wos_slots[sl2]])
                        wos_slots[sl2] = b_wn
                        b_wsg = Buf(f"wsg{qbg}{half}{ok}")
                        S.claim(b_wsg, [b_wostag[sl2]])
                        b_wostag[sl2] = b_wsg
                        dma("sp" if sl2 == 0 else "act",
                            wostag[sl2][:, :],
                            wo_d[ok * P:(ok + 1) * P,
                                 half * 1024:(half + 1) * 1024],
                            writes=[b_wsg], tag="lwo", chan=f"wo{sl2}")
                        dve("tensor_copy",
                            wos_ap[:, sl2 * 1024:sl2 * 1024 + 1024],
                            wostag[sl2][:, :],
                            reads=[b_wsg], writes=[b_wn], tag="wocast")
                        for eo8 in range(8):
                            mm(wop_ps[eo8],
                               wos_ap[:, sl2 * 1024 + eo8 * P:
                                      sl2 * 1024 + (eo8 + 1) * P],
                               og_ap(ok)[:, 0:512],
                               start=(ok == 0), stop=(ok == 7),
                               reads=[b_wn, b_og], writes=[wop_pb[eo8]],
                               tag="wop", commit=True)
                    for eo8 in range(8):
                        eo = half * 8 + eo8
                        sl = eo8 % 2
                        dve("tensor_copy", stg_ap[:, sl * 512:(sl + 1) * 512],
                            wop_ps[eo8],
                            reads=[wop_pb[eo8], wop_pb[7]], writes=[b_stg],
                            tag="wos")
                        if ("parta" in taps and half == 1 and eo8 < 2
                                and qbg in (1, 2)):
                            dsl = (qbg - 1) * 2 + eo8
                            dve("tensor_copy",
                                dbgs[:, dsl * 512:(dsl + 1) * 512],
                                stg_ap[:, sl * 512:(sl + 1) * 512],
                                reads=[b_stg], writes=[b_dbgs], tag="dsn")

                        dma("pool",
                            parta[eo * P:(eo + 1) * P,
                                  qbg * 512:(qbg + 1) * 512],
                            stg_ap[:, sl * 512:(sl + 1) * 512],
                            reads=[b_stg], writes=[b_parta], tag="wod")
        if "parta" in taps:
            tp = tapout("parta", (E, T))
            dma("pool", tp[:, :], parta[:, :], reads=[b_parta], tag="tpa")
            tps = tapout("stgsnap", (P, 2048))
            dma("pool", tps[:, :], dbgs[:, :], reads=[b_dbgs], tag="tsn")
        for k in range(4):
            dyn_dma(k, 2 * E - 512,
                    (lambda off: xch_a[bass.ds(off, 512), :]),
                    parta[k * 512:(k + 1) * 512, :],
                    reads=[b_parta], writes=[b_xch_a], tag="px")
        return [wos_slots[0], wos_slots[1], b_qtb, b_e, b_at, b_og, b_osq,
                b_rb, b_stg, b_kt, b_v, b_ws, b_qst] + prev_xt + ln_scr
    attn_bufs = emit_attn()

    def cc_bar0(e):
        return e.collective_compute(
            "AllGather", ALU.bypass, replica_groups=RGALL,
            ins=[bar_in.ap().opt()], outs=[bar_out[0].ap().opt()])
    S.collective(cc_bar0, reads=[b_xch_a], writes=[b_xch_a], tag="bar0")

    def emit_attn_merge():
        mg_ap = af32(90, 2048)
        b_mg = Buf("mg")
        S.claim(b_mg, [attn_bufs[3]])
        for rk in range(4):
            dyn_dma_in(16, T - TL, partb[rk * TL:(rk + 1) * TL, :],
                       (lambda off, rk=rk:
                        xch_a[rk * TL:(rk + 1) * TL, bass.ds(off, TL)]),
                       reads=[b_xch_a], writes=[b_partb], tag="mgc")
        for f in range(NF):
            for sl in range(2):
                dma("pool", mg_ap[:, sl * TL:(sl + 1) * TL],
                    partb[sl * E + f * P:sl * E + (f + 1) * P, :],
                    reads=[b_partb], writes=[b_mg], tag="mgr")
            dve("tensor_tensor", res_t[:, f * TL:(f + 1) * TL],
                res_t[:, f * TL:(f + 1) * TL], mg_ap[:, 0:TL], ALU.add,
                reads=[b_mg], writes=[b_res], tag="mg1")
            dve("tensor_tensor", res_t[:, f * TL:(f + 1) * TL],
                res_t[:, f * TL:(f + 1) * TL], mg_ap[:, TL:2 * TL], ALU.add,
                reads=[b_mg], writes=[b_res], tag="mg2")
        return [b_mg]
    merge_bufs = emit_attn_merge()
    if "res_a" in taps:
        tp = tapout("res_a", (E, TL))
        for f in range(NF):
            dma("pool", tp[f * P:(f + 1) * P, :], res_t[:, f * TL:(f + 1) * TL],
                reads=[b_res], tag="tra")

    def emit_ffn(idx, prev):
        int_ap = lambda io: abf(0 + io, 512)
        gu_ap = abf(64, 16384)
        xnf_ap = lambda f: abf(96 + f, 512)
        b_int = Buf(f"int{idx}")
        b_gug = Buf(f"gug{idx}")
        b_guu = Buf(f"guu{idx}")
        b_xnf = [Buf(f"fx{idx}_{f}") for f in range(NF)]
        S.claim(b_int, prev)
        S.claim(b_gug, prev)
        S.claim(b_guu, prev)
        for bx in b_xnf:
            S.claim(bx, prev)
        sg_ap = abf(115, 512)
        ug_ap = abf(116, 512)
        b_sg, b_ug = Buf(f"sg{idx}"), Buf(f"ug{idx}")
        S.claim(b_sg, prev)
        S.claim(b_ug, prev)
        scr = prev
        ret_scr = []
        for tch in range(2):
            c0 = tch * 512
            for f in range(NF):
                dve("tensor_copy", xnf_ap(f),
                    res_t[:, f * TL + c0:f * TL + c0 + 512],
                    reads=[b_res], writes=[b_xnf[f]], tag="fc")
            scr = emit_ln_chunk(xnf_ap, b_xnf, f"f{idx}{tch}", 112, scr)
            ret_scr = scr
            for ic in range(16):
                bgn = Buf(f"g{idx}{tch}{ic}")
                S.claim(bgn, [b_gug])
                b_gug = bgn
                bun = Buf(f"u{idx}{tch}{ic}")
                S.claim(bun, [b_guu])
                b_guu = bun
                for gi, base, wb in ((0, 0, b_gug), (1, 256, b_guu)):
                    for r in range(8):
                        dma("sp" if r % 2 == 0 else "act",
                            gu_ap[:, gi * 8192 + r * 1024:
                                  gi * 8192 + (r + 1) * 1024].rearrange(
                                      "p (h i) -> p h i", h=2),
                            wg[768 * r + base:768 * r + base + 256,
                               ic * 512:(ic + 1) * 512].rearrange(
                                   "(h p) i -> p h i", p=P),
                            reads=[b_wg], writes=[wb], tag="gld",
                            chan="gw" if gi == 0 else "uw")
                for io in range(4):
                    iog = ic * 4 + io
                    pg = ps1[:, (io % 2) * 512:(io % 2) * 512 + 512]
                    bg = b_ps1[io % 2]
                    pu = ps1[:, 1024 + (io % 2) * 512:
                             1024 + (io % 2) * 512 + 512]
                    bu = b_ps1[2 + io % 2]
                    for hb in range(16):
                        gcol = (hb // 2) * 1024 + (hb % 2) * 512 + io * P
                        mm(pg, gu_ap[:, gcol:gcol + P], xnf_ap(hb),
                           start=(hb == 0), stop=(hb == 15),
                           reads=[b_gug, b_xnf[hb]], writes=[bg], tag="gmm")
                    for hb in range(16):
                        ucol = 8192 + (hb // 2) * 1024 + (hb % 2) * 512 + io * P
                        mm(pu, gu_ap[:, ucol:ucol + P], xnf_ap(hb),
                           start=(hb == 0), stop=(hb == 15),
                           reads=[b_guu, b_xnf[hb]], writes=[bu], tag="umm")
                    act(sg_ap[:, :], pg, AF.Silu, reads=[bg], writes=[b_sg],
                        tag="sil")
                    dve("tensor_copy", ug_ap[:, :], pu, reads=[bu],
                        writes=[b_ug], tag="ue")
                    dve("tensor_tensor", int_ap(iog)[:, :], sg_ap[:, :],
                        ug_ap[:, :], ALU.mult, reads=[b_sg, b_ug],
                        writes=[b_int], tag="im")
            for ht in range(16):
                r, sub = divmod(ht, 2)
                if ht % 2 == 0:
                    bdw = Buf(f"d{idx}{tch}{ht}")
                    S.claim(bdw, [b_gug])
                    b_gug = bdw
                else:
                    bdw = Buf(f"d{idx}{tch}{ht}")
                    S.claim(bdw, [b_guu])
                    b_guu = bdw
                dma("sp" if ht % 2 == 0 else "act",
                    gu_ap[:, (ht % 2) * 8192:(ht % 2) * 8192 + 8192],
                    wg[768 * r + 512 + sub * P:768 * r + 512 + sub * P + P, :],
                    reads=[b_wg], writes=[bdw], tag="dld",
                    chan=f"dw{ht % 2}")
                pd = ps2[:, (ht % 2) * 512:(ht % 2) * 512 + 512]
                bd = b_ps2[ht % 2]
                for it in range(64):
                    mm(pd, gu_ap[:, (ht % 2) * 8192 + it * P:
                                 (ht % 2) * 8192 + (it + 1) * P],
                       int_ap(it)[:, :], start=(it == 0), stop=(it == 63),
                       reads=[bdw, b_int], writes=[bd], tag="dmm")
                dve("tensor_tensor",
                    res_t[:, ht * TL + c0:ht * TL + c0 + 512],
                    res_t[:, ht * TL + c0:ht * TL + c0 + 512], pd, ALU.add,
                    reads=[bd], writes=[b_res], tag="dac")
        return [b_int, b_gug, b_guu, b_sg, b_ug] + b_xnf + ret_scr

    ffn1_bufs = emit_ffn(0, attn_bufs + merge_bufs)
    if "res_f1" in taps:
        tp = tapout("res_f1", (E, TL))
        for f in range(NF):
            dma("pool", tp[f * P:(f + 1) * P, :], res_t[:, f * TL:(f + 1) * TL],
                reads=[b_res], tag="trf")

    def emit_attnb(prev):
        xh_ap = lambda f: abf(0 + 2 * f, 1024)
        xnb_ap = lambda f: abf(32 + 2 * f, 1024)
        hm_ap = xnb_ap
        PLB = {"ga": 64, "lr": 72, "r": 80, "S": 88, "Phi": 96}
        pla = lambda nm, ct: af32(PLB[nm] + ct, 256)
        gx_ap = lambda ct: abf(104, 4096)[:, ct * 256:(ct + 1) * 256]
        tmp = lambda i: abf(108, 4096)[:, i * 256:(i + 1) * 256]
        wsl_ap = abf(64, 16384)
        wgz_ap = abf(112, 4096)
        b_xh = [Buf(f"xh{f}") for f in range(NF)]
        b_xnb = [Buf(f"bx{f}") for f in range(NF)]
        b_hm = [Buf(f"hm{f}") for f in range(NF)]
        b_pl, b_gx, b_tmp, b_wgz = Buf("pl"), Buf("gx"), Buf("tmp"), Buf("wgz")
        b_wsl = Buf("wsl")
        for bx in b_xh + b_xnb + [b_gx, b_wgz, b_wsl]:
            S.claim(bx, prev)
        scr = prev
        for tch in range(2):
            c0 = tch * 512
            for f in range(NF):
                dve("tensor_copy", xnb_ap(f)[:, c0:c0 + 512],
                    res_t[:, f * TL + c0:f * TL + c0 + 512],
                    reads=[b_res], writes=[b_xnb[f]], tag="bc")
            scr = emit_ln_chunk(
                (lambda cc0: (lambda f: xnb_ap(f)[:, cc0:cc0 + 512]))(c0),
                b_xnb, f"b{tch}", 108, scr)
        S.claim(b_tmp, scr + [b_pl])
        for dh in range(2):
            b_wn = Buf(f"wi{dh}")
            S.claim(b_wn, [b_wsl])
            b_wsl = b_wn
            for ek in range(16):
                dma("pool",
                    wsl_ap[:, ek * 1024:(ek + 1) * 1024],
                    win_d[ek * P:(ek + 1) * P, dh * 1024:(dh + 1) * 1024],
                    writes=[b_wsl], tag="wld", chan="wi")
            for do in range(8):
                dof = dh * 8 + do
                for tch in range(2):
                    pslice = ps1[:, (tch % 2) * 512:(tch % 2) * 512 + 512]
                    pbuf = b_ps1[tch % 2]
                    for ek in range(16):
                        mm(pslice, wsl_ap[:, ek * 1024 + do * P:
                                          ek * 1024 + (do + 1) * P],
                           xnb_ap(ek)[:, tch * 512:tch * 512 + 512],
                           start=(ek == 0), stop=(ek == 15),
                           reads=[b_wsl, b_xnb[ek]], writes=[pbuf], tag="xm")
                    dve("tensor_scalar",
                        xh_ap(dof)[:, tch * 512:tch * 512 + 512], pslice,
                        smalls[:, SM_BIN + dof:SM_BIN + dof + 1], None,
                        ALU.add,
                        reads=[pbuf, b_smalls], writes=[b_xh[dof]], tag="xe")
        if "xh" in taps:
            tp = tapout("xh", (E, TL), BF16)
            for f in range(NF):
                dma("pool", tp[f * P:(f + 1) * P, :], xh_ap(f),
                    reads=[b_xh[f]], tag="txh")
        for wd, off in ((wgx_d, 0), (wga_d, 2048)):
            for i in range(16):
                dma("pool", wgz_ap[:, off + i * P:off + (i + 1) * P],
                    wd[i * P:(i + 1) * P, :], writes=[b_wgz], tag="wgz",
                    chan="wgz")
        for f in range(NF):
            S.claim(b_hm[f], [b_xnb[f], b_wsl])
        S.claim(b_pl, [b_wsl] + list(prev))
        for ck in range(4):
            c0 = ck * 256
            for blk in range(8):
                for k2 in range(2):
                    mm(ps3[:, 0:256], wgz_ap[:, 2048 + (blk * 2 + k2) * P:
                                             2048 + (blk * 2 + k2 + 1) * P],
                       xh_ap(blk * 2 + k2)[:, c0:c0 + 256],
                       start=(k2 == 0), stop=(k2 == 1),
                       reads=[b_wgz, b_xh[blk * 2 + k2]], writes=[b_ps3],
                       tag="gam")
                act(pla("ga", blk)[:, :], ps3[:, 0:256], AF.Sigmoid,
                    bias=smalls[:, SM_BGA + blk:SM_BGA + blk + 1],
                    reads=[b_ps3, b_smalls], writes=[b_pl], tag="gae")
                for k2 in range(2):
                    mm(ps3[:, 256:512], wgz_ap[:, (blk * 2 + k2) * P:
                                               (blk * 2 + k2 + 1) * P],
                       xh_ap(blk * 2 + k2)[:, c0:c0 + 256],
                       start=(k2 == 0), stop=(k2 == 1),
                       reads=[b_wgz, b_xh[blk * 2 + k2]], writes=[b_ps3],
                       tag="gxm")
                act(gx_ap(blk)[:, :], ps3[:, 256:512], AF.Sigmoid,
                    bias=smalls[:, SM_BGX + blk:SM_BGX + blk + 1],
                    reads=[b_ps3, b_smalls], writes=[b_gx], tag="gxe")
            for ct in range(8):
                A = pla("ga", ct)
                BB = pla("lr", ct)
                TH = pla("S", ct)
                PH = pla("Phi", ct)
                dve("tensor_scalar", BB[:, :], A[:, :],
                    smalls[:, SM_N8SP + ct:SM_N8SP + ct + 1], None, ALU.mult,
                    reads=[b_pl, b_smalls], writes=[b_pl], tag="lr")
                dve("tensor_scalar", TH[:, :], A[:, :],
                    smalls[:, SM_AI + ct:SM_AI + ct + 1], None, ALU.mult,
                    reads=[b_pl, b_smalls], writes=[b_pl], tag="th")
                dve("tensor_tensor_scan", PH[:, :], ones256[:, :], TH[:, :],
                    carry[:, ct:ct + 1], ALU.mult, ALU.add,
                    reads=[b_pl, b_carry, b_ones], writes=[b_pl], tag="cph")
                dve("tensor_copy", carry[:, ct:ct + 1], PH[:, 255:256],
                    reads=[b_pl], writes=[b_carry], tag="cphc")
                dve("tensor_scalar", A[:, :], TH[:, :], 1.0 / math.pi, MAGIC,
                    ALU.mult, ALU.add, reads=[b_pl], writes=[b_pl], tag="q1")
                dve("tensor_scalar", A[:, :], A[:, :], MAGIC, None,
                    ALU.subtract, reads=[b_pl], writes=[b_pl], tag="q2")
                dve("scalar_tensor_tensor", A[:, :], A[:, :], -math.pi,
                    TH[:, :], ALU.mult, ALU.add, reads=[b_pl], writes=[b_pl],
                    tag="q3")
                act(tmp(1)[:, :], A[:, :], AF.Sin, scale=2.0,
                    reads=[b_pl], writes=[b_tmp], tag="s2t")
                dve("tensor_scalar", A[:, :], TH[:, :], 1.0 / math.pi,
                    MAGIC + 0.25, ALU.mult, ALU.add, reads=[b_pl],
                    writes=[b_pl], tag="q4")
                dve("tensor_scalar", A[:, :], A[:, :], MAGIC, None,
                    ALU.subtract, reads=[b_pl], writes=[b_pl], tag="q5")
                dve("scalar_tensor_tensor", A[:, :], A[:, :], -math.pi,
                    TH[:, :], ALU.mult, ALU.add, reads=[b_pl], writes=[b_pl],
                    tag="q6")
                act(tmp(0)[:, :], A[:, :], AF.Sin, scale=2.0,
                    bias=smalls[:, SM_PIH:SM_PIH + 1],
                    reads=[b_pl], writes=[b_tmp], tag="c2t")
                dve("tensor_scalar", A[:, :], PH[:, :], 1.0 / TWO_PI, MAGIC,
                    ALU.mult, ALU.add, reads=[b_pl], writes=[b_pl], tag="q7")
                dve("tensor_scalar", A[:, :], A[:, :], MAGIC, None,
                    ALU.subtract, reads=[b_pl], writes=[b_pl], tag="q8")
                dve("scalar_tensor_tensor", A[:, :], A[:, :], -TWO_PI,
                    PH[:, :], ALU.mult, ALU.add, reads=[b_pl], writes=[b_pl],
                    tag="q9")
                act(tmp(6)[:, :], A[:, :], AF.Sin,
                    reads=[b_pl], writes=[b_tmp], tag="snp")
                dve("tensor_scalar", A[:, :], PH[:, :], 1.0 / TWO_PI,
                    MAGIC + 0.25, ALU.mult, ALU.add, reads=[b_pl],
                    writes=[b_pl], tag="qa")
                dve("tensor_scalar", A[:, :], A[:, :], MAGIC, None,
                    ALU.subtract, reads=[b_pl], writes=[b_pl], tag="qb")
                dve("scalar_tensor_tensor", A[:, :], A[:, :], -TWO_PI,
                    PH[:, :], ALU.mult, ALU.add, reads=[b_pl], writes=[b_pl],
                    tag="qc")
                act(tmp(7)[:, :], A[:, :], AF.Sin,
                    bias=smalls[:, SM_PIH:SM_PIH + 1],
                    reads=[b_pl], writes=[b_tmp], tag="csp")
                dve("tensor_tensor_scan", TH[:, :], ones256[:, :], BB[:, :],
                    carry[:, 8 + ct:9 + ct], ALU.mult, ALU.add,
                    reads=[b_pl, b_carry, b_ones], writes=[b_pl], tag="cs")
                dve("tensor_copy", carry[:, 8 + ct:9 + ct], TH[:, 255:256],
                    reads=[b_pl], writes=[b_carry], tag="csc")
                act(pla("r", ct)[:, :], BB[:, :], AF.Exp,
                    reads=[b_pl], writes=[b_pl], tag="re")
                act(A[:, :], BB[:, :], AF.Exp, scale=2.0,
                    reads=[b_pl], writes=[b_pl], tag="aa")
                dve("tensor_tensor", BB[:, :], A[:, :], tmp(0)[:, :],
                    ALU.mult, reads=[b_pl, b_tmp], writes=[b_pl], tag="z1")
                dve("tensor_scalar", BB[:, :], BB[:, :], -1.0, 1.0, ALU.mult,
                    ALU.add, reads=[b_pl], writes=[b_pl], tag="z2")
                dve("tensor_tensor", tmp(2)[:, :], A[:, :], tmp(1)[:, :],
                    ALU.mult, reads=[b_pl, b_tmp], writes=[b_tmp], tag="z3")
                act(A[:, :], BB[:, :], AF.Square, reads=[b_pl], writes=[b_pl],
                    tag="m1")
                act(tmp(3)[:, :], tmp(2)[:, :], AF.Square,
                    reads=[b_tmp], writes=[b_tmp], tag="m2")
                dve("tensor_tensor", A[:, :], A[:, :], tmp(3)[:, :], ALU.add,
                    reads=[b_pl, b_tmp], writes=[b_pl], tag="m3")
                act(A[:, :], A[:, :], AF.Ln,
                    bias=smalls[:, SM_TINY:SM_TINY + 1],
                    reads=[b_pl], writes=[b_pl], tag="m4")
                act(A[:, :], A[:, :], AF.Exp, scale=0.5,
                    reads=[b_pl], writes=[b_pl], tag="m5")
                dve("tensor_tensor", tmp(4)[:, :], A[:, :], BB[:, :], ALU.add,
                    reads=[b_pl], writes=[b_tmp], tag="sr1")
                dve("tensor_scalar_max", tmp(4)[:, :], tmp(4)[:, :], 0.0,
                    reads=[b_tmp], writes=[b_tmp], tag="sr2")
                act(tmp(4)[:, :], tmp(4)[:, :], AF.Ln,
                    bias=smalls[:, SM_TINY:SM_TINY + 1], scale=0.5,
                    reads=[b_tmp], writes=[b_tmp], tag="sr3")
                act(tmp(4)[:, :], tmp(4)[:, :], AF.Exp, scale=0.5,
                    reads=[b_tmp], writes=[b_tmp], tag="sr4")
                dve("tensor_tensor", tmp(5)[:, :], A[:, :], BB[:, :],
                    ALU.subtract, reads=[b_pl], writes=[b_tmp], tag="si1")
                dve("tensor_scalar_max", tmp(5)[:, :], tmp(5)[:, :], 0.0,
                    reads=[b_tmp], writes=[b_tmp], tag="si2")
                act(tmp(5)[:, :], tmp(5)[:, :], AF.Ln,
                    bias=smalls[:, SM_TINY:SM_TINY + 1], scale=0.5,
                    reads=[b_tmp], writes=[b_tmp], tag="si3")
                act(tmp(5)[:, :], tmp(5)[:, :], AF.Exp, scale=0.5,
                    reads=[b_tmp], writes=[b_tmp], tag="si4")
                act(tmp(3)[:, :], tmp(2)[:, :], AF.Sign,
                    reads=[b_tmp], writes=[b_tmp], tag="sg1")
                dve("tensor_scalar", tmp(3)[:, :], tmp(3)[:, :], -1.0, None,
                    ALU.mult, reads=[b_tmp], writes=[b_tmp], tag="sg2")
                dve("tensor_tensor", tmp(5)[:, :], tmp(5)[:, :], tmp(3)[:, :],
                    ALU.mult, reads=[b_tmp], writes=[b_tmp], tag="sg3")
                dve("tensor_tensor", tmp(2)[:, :], gx_ap(ct)[:, :],
                    xh_ap(ct)[:, c0:c0 + 256], ALU.mult,
                    reads=[b_gx, b_xh[ct]], writes=[b_tmp], tag="xp1")
                dve("tensor_tensor", tmp(3)[:, :], gx_ap(ct)[:, :],
                    xh_ap(8 + ct)[:, c0:c0 + 256], ALU.mult,
                    reads=[b_gx, b_xh[8 + ct]], writes=[b_tmp], tag="xp2")
                dve("tensor_tensor", tmp(1)[:, :], tmp(4)[:, :], tmp(2)[:, :],
                    ALU.mult, reads=[b_tmp], writes=[b_tmp], tag="c1")
                dve("tensor_tensor", tmp(0)[:, :], tmp(5)[:, :], tmp(3)[:, :],
                    ALU.mult, reads=[b_tmp], writes=[b_tmp], tag="c2")
                dve("tensor_tensor", tmp(1)[:, :], tmp(1)[:, :], tmp(0)[:, :],
                    ALU.subtract, reads=[b_tmp], writes=[b_tmp], tag="c3")
                dve("tensor_tensor", tmp(0)[:, :], tmp(4)[:, :], tmp(3)[:, :],
                    ALU.mult, reads=[b_tmp], writes=[b_tmp], tag="c4")
                dve("tensor_tensor", tmp(2)[:, :], tmp(5)[:, :], tmp(2)[:, :],
                    ALU.mult, reads=[b_tmp], writes=[b_tmp], tag="c5")
                dve("tensor_tensor", tmp(0)[:, :], tmp(0)[:, :], tmp(2)[:, :],
                    ALU.add, reads=[b_tmp], writes=[b_tmp], tag="c6")
                dve("tensor_tensor", tmp(2)[:, :], tmp(7)[:, :], tmp(1)[:, :],
                    ALU.mult, reads=[b_tmp], writes=[b_tmp], tag="u1")
                dve("tensor_tensor", tmp(3)[:, :], tmp(6)[:, :], tmp(0)[:, :],
                    ALU.mult, reads=[b_tmp], writes=[b_tmp], tag="u2")
                dve("tensor_tensor", tmp(2)[:, :], tmp(2)[:, :], tmp(3)[:, :],
                    ALU.add, reads=[b_tmp], writes=[b_tmp], tag="u3")
                dve("tensor_tensor", tmp(3)[:, :], tmp(7)[:, :], tmp(0)[:, :],
                    ALU.mult, reads=[b_tmp], writes=[b_tmp], tag="u4")
                dve("tensor_tensor", tmp(4)[:, :], tmp(6)[:, :], tmp(1)[:, :],
                    ALU.mult, reads=[b_tmp], writes=[b_tmp], tag="u5")
                dve("tensor_tensor", tmp(3)[:, :], tmp(3)[:, :], tmp(4)[:, :],
                    ALU.subtract, reads=[b_tmp], writes=[b_tmp], tag="u6")
                dve("tensor_tensor_scan", BB[:, :], pla("r", ct)[:, :],
                    tmp(2)[:, :], carry[:, 16 + ct:17 + ct], ALU.mult, ALU.add,
                    reads=[b_pl, b_tmp, b_carry], writes=[b_pl], tag="wr")
                dve("tensor_copy", carry[:, 16 + ct:17 + ct], BB[:, 255:256],
                    reads=[b_pl], writes=[b_carry], tag="wrc")
                dve("tensor_tensor_scan", A[:, :], pla("r", ct)[:, :],
                    tmp(3)[:, :], carry[:, 24 + ct:25 + ct], ALU.mult, ALU.add,
                    reads=[b_pl, b_tmp, b_carry], writes=[b_pl], tag="wi")
                dve("tensor_copy", carry[:, 24 + ct:25 + ct], A[:, 255:256],
                    reads=[b_pl], writes=[b_carry], tag="wic")
                dve("tensor_tensor", tmp(4)[:, :], tmp(7)[:, :], BB[:, :],
                    ALU.mult, reads=[b_pl, b_tmp], writes=[b_tmp], tag="h1")
                dve("tensor_tensor", tmp(5)[:, :], tmp(6)[:, :], A[:, :],
                    ALU.mult, reads=[b_pl, b_tmp], writes=[b_tmp], tag="h2")
                dve("tensor_tensor", hm_ap(ct)[:, c0:c0 + 256], tmp(4)[:, :],
                    tmp(5)[:, :], ALU.subtract,
                    reads=[b_tmp], writes=[b_hm[ct]], tag="h3")
                dve("tensor_tensor", tmp(4)[:, :], tmp(6)[:, :], BB[:, :],
                    ALU.mult, reads=[b_pl, b_tmp], writes=[b_tmp], tag="h4")
                dve("tensor_tensor", tmp(5)[:, :], tmp(7)[:, :], A[:, :],
                    ALU.mult, reads=[b_pl, b_tmp], writes=[b_tmp], tag="h5")
                dve("tensor_tensor", hm_ap(8 + ct)[:, c0:c0 + 256],
                    tmp(4)[:, :], tmp(5)[:, :], ALU.add,
                    reads=[b_tmp], writes=[b_hm[8 + ct]], tag="h6")
                act(tmp(4)[:, :], TH[:, :], AF.Exp,
                    reads=[b_pl], writes=[b_tmp], tag="pse")
                dve("tensor_tensor", tmp(5)[:, :], tmp(4)[:, :], tmp(7)[:, :],
                    ALU.mult, reads=[b_tmp], writes=[b_tmp], tag="pr")
                dma("sp", pdram[ct * P:(ct + 1) * P, c0:c0 + 256],
                    tmp(5)[:, :], reads=[b_tmp], writes=[b_pdram], tag="pd1",
                    chan="pds")
                dve("tensor_tensor", tmp(5)[:, :], tmp(4)[:, :], tmp(6)[:, :],
                    ALU.mult, reads=[b_tmp], writes=[b_tmp], tag="pi")
                dma("sp", pdram[TL + ct * P:TL + (ct + 1) * P, c0:c0 + 256],
                    tmp(5)[:, :], reads=[b_tmp], writes=[b_pdram], tag="pd2",
                    chan="pds")
        for ct in range(8):
            dve("tensor_copy", hend[:, 2 * ct:2 * ct + 1],
                hm_ap(ct)[:, TL - 1:TL], reads=[b_hm[ct]], writes=[b_hend],
                tag="he1")
            dve("tensor_copy", hend[:, 2 * ct + 1:2 * ct + 2],
                hm_ap(8 + ct)[:, TL - 1:TL], reads=[b_hm[8 + ct]],
                writes=[b_hend], tag="he2")
        for ct in range(8):
            dma("pool", partc[ct * P:(ct + 1) * P, :],
                hend[:, 2 * ct:2 * ct + 2],
                reads=[b_hend], writes=[b_partc], tag="hw")
        dyn_dma(17, 2 * TL - TL,
                (lambda off: xch_b[bass.ds(off, TL), :]), partc[:, :],
                reads=[b_partc], writes=[b_xch_b], tag="hwc")

        def cc_bar1(e):
            return e.collective_compute(
                "AllGather", ALU.bypass, replica_groups=RGALL,
                ins=[bar_in.ap().opt()], outs=[bar_out[1].ap().opt()])
        S.collective(cc_bar1, reads=[b_xch_b], writes=[b_xch_b], tag="bar1")
        dyn_dma_in(25, 2 * TL - TL, partd[:, :],
                   (lambda off: xch_b[bass.ds(off, TL), :]),
                   reads=[b_xch_b], writes=[b_partd], tag="hrc")
        for ct in range(8):
            dma("pool", hin[:, 2 * ct:2 * ct + 2],
                partd[ct * P:(ct + 1) * P, :],
                reads=[b_partd], writes=[b_hin], tag="hr")
        dve("tensor_scalar", hin[:, :], hin[:, :],
            smalls[:, SM_HM:SM_HM + 1], None, ALU.mult,
            reads=[b_smalls], writes=[b_hin], tag="hmm")
        for ck in range(4):
            c0 = ck * 256
            for ct in range(8):
                dma("pool", tmp(0)[:, :],
                    pdram[ct * P:(ct + 1) * P, c0:c0 + 256],
                    reads=[b_pdram], writes=[b_tmp], tag="fp1")
                dma("pool", tmp(1)[:, :],
                    pdram[TL + ct * P:TL + (ct + 1) * P, c0:c0 + 256],
                    reads=[b_pdram], writes=[b_tmp], tag="fp2")
                dve("tensor_scalar", tmp(2)[:, :], tmp(0)[:, :],
                    hin[:, 2 * ct:2 * ct + 1], None, ALU.mult,
                    reads=[b_tmp, b_hin], writes=[b_tmp], tag="f1")
                dve("tensor_scalar", tmp(3)[:, :], tmp(1)[:, :],
                    hin[:, 2 * ct + 1:2 * ct + 2], None, ALU.mult,
                    reads=[b_tmp, b_hin], writes=[b_tmp], tag="f2")
                dve("tensor_tensor", tmp(2)[:, :], tmp(2)[:, :], tmp(3)[:, :],
                    ALU.subtract, reads=[b_tmp], writes=[b_tmp], tag="f3")
                dve("tensor_tensor", hm_ap(ct)[:, c0:c0 + 256],
                    hm_ap(ct)[:, c0:c0 + 256], tmp(2)[:, :], ALU.add,
                    reads=[b_tmp], writes=[b_hm[ct]], tag="f4")
                dve("tensor_scalar", tmp(2)[:, :], tmp(1)[:, :],
                    hin[:, 2 * ct:2 * ct + 1], None, ALU.mult,
                    reads=[b_tmp, b_hin], writes=[b_tmp], tag="f5")
                dve("tensor_scalar", tmp(3)[:, :], tmp(0)[:, :],
                    hin[:, 2 * ct + 1:2 * ct + 2], None, ALU.mult,
                    reads=[b_tmp, b_hin], writes=[b_tmp], tag="f6")
                dve("tensor_tensor", tmp(2)[:, :], tmp(2)[:, :], tmp(3)[:, :],
                    ALU.add, reads=[b_tmp], writes=[b_tmp], tag="f7")
                dve("tensor_tensor", hm_ap(8 + ct)[:, c0:c0 + 256],
                    hm_ap(8 + ct)[:, c0:c0 + 256], tmp(2)[:, :], ALU.add,
                    reads=[b_tmp], writes=[b_hm[8 + ct]], tag="f8")
        b_wsl2 = Buf("wo2")
        S.claim(b_wsl2, [b_pl])
        for dh in range(2):
            if dh == 1:
                bw = Buf("wo2b")
                S.claim(bw, [b_wsl2])
                b_wsl2 = bw
            for ek in range(16):
                dma("pool",
                    wsl_ap[:, ek * 1024:(ek + 1) * 1024],
                    wout_d[ek * P:(ek + 1) * P, dh * 1024:(dh + 1) * 1024],
                    writes=[b_wsl2], tag="wod")
            for do in range(8):
                dof = dh * 8 + do
                for tch in range(2):
                    pslice = ps1[:, (tch % 2) * 512:(tch % 2) * 512 + 512]
                    pbuf = b_ps1[tch % 2]
                    for ek in range(16):
                        mm(pslice, wsl_ap[:, ek * 1024 + do * P:
                                          ek * 1024 + (do + 1) * P],
                           hm_ap(ek)[:, tch * 512:tch * 512 + 512],
                           start=(ek == 0), stop=(ek == 15),
                           reads=[b_wsl2, b_hm[ek]], writes=[pbuf], tag="om")
                    dve("scalar_tensor_tensor",
                        res_t[:, dof * TL + tch * 512:
                              dof * TL + tch * 512 + 512],
                        pslice, smalls[:, SM_BOUT + dof:SM_BOUT + dof + 1],
                        res_t[:, dof * TL + tch * 512:
                              dof * TL + tch * 512 + 512],
                        ALU.add, ALU.add,
                        reads=[pbuf, b_smalls], writes=[b_res], tag="oe")
        return b_xh + b_hm + [b_pl, b_gx, b_tmp, b_wgz, b_wsl2]

    attnb_bufs = emit_attnb(ffn1_bufs)
    if "res_b" in taps:
        tp = tapout("res_b", (E, TL))
        for f in range(NF):
            dma("pool", tp[f * P:(f + 1) * P, :], res_t[:, f * TL:(f + 1) * TL],
                reads=[b_res], tag="trb")

    emit_ffn(1, attnb_bufs)

    b_outd = Buf("outd")
    for f in range(NF):
        dma("sp", out_res[f * P:(f + 1) * P, :], res_t[:, f * TL:(f + 1) * TL],
            reads=[b_res], writes=[b_outd], tag="out", chan="out")
    S.op("sp", lambda e: e.nop(), reads=[b_outd], tag="fin")

    blk = es.enter_context(nc.Block())
    S.build(nc, es, blk)
    es.close()
    return nc, list(tap_outs.keys())



def _host_inputs(inputs, core, consts):
    b, par = core // 2, core % 2
    CmatT, u_lhs, b_lhs, a_lhs = consts
    x = np.asarray(inputs["L0_attn_a_x"])
    d = {}
    d["x_lo"] = np.ascontiguousarray(x[b, 0:TL, :].T)
    d["x_hi"] = np.ascontiguousarray(x[b, TL:, :].T)
    d["x_res"] = d["x_lo"] if par == 0 else d["x_hi"]
    cs = par * 1024
    d["wq"] = np.ascontiguousarray(np.asarray(inputs["L0_attn_a_Wq"])[:, cs:cs + 1024])
    d["wk"] = np.ascontiguousarray(np.asarray(inputs["L0_attn_a_Wk"])[:, cs:cs + 1024])
    d["wv"] = np.ascontiguousarray(np.asarray(inputs["L0_attn_a_Wv"])[:, cs:cs + 1024])
    d["wo"] = np.ascontiguousarray(np.asarray(inputs["L0_attn_a_Wo"])[cs:cs + 1024, :])
    d["win"] = np.ascontiguousarray(np.asarray(inputs["L0_attn_b_W_in"]))
    d["wout"] = np.ascontiguousarray(np.asarray(inputs["L0_attn_b_W_out"]))
    d["wgx"] = np.ascontiguousarray(
        np.asarray(inputs["L0_attn_b_W_gx"]).reshape(16 * P, P))
    d["wga"] = np.ascontiguousarray(
        np.asarray(inputs["L0_attn_b_W_ga"]).reshape(16 * P, P))
    sm = np.zeros((P, 96), np.float32)
    sm[:, SM_LAM + 0] = np.asarray(inputs["L0_attn_a_lambda_q1"])
    sm[:, SM_LAM + 1] = np.asarray(inputs["L0_attn_a_lambda_k1"])
    sm[:, SM_LAM + 2] = np.asarray(inputs["L0_attn_a_lambda_q2"])
    sm[:, SM_LAM + 3] = np.asarray(inputs["L0_attn_a_lambda_k2"])
    g = np.asarray(inputs["L0_attn_a_g"])
    sm[:, SM_G + 0] = g[0:P]
    sm[:, SM_G + 1] = g[P:2 * P]
    sm[:, SM_BIN:SM_BIN + NF] = np.asarray(inputs["L0_attn_b_b_in"]).reshape(NF, P).T
    sm[:, SM_BOUT:SM_BOUT + NF] = np.asarray(inputs["L0_attn_b_b_out"]).reshape(NF, P).T
    sm[:, SM_BGX:SM_BGX + 8] = np.asarray(inputs["L0_attn_b_b_gx"]).T
    sm[:, SM_BGA:SM_BGA + 8] = np.asarray(inputs["L0_attn_b_b_ga"]).T
    sm[:, SM_AR:SM_AR + 8] = np.asarray(inputs["L0_attn_b_a_real"]).reshape(8, P).T
    sm[:, SM_AI:SM_AI + 8] = np.asarray(inputs["L0_attn_b_a_imag"]).reshape(8, P).T
    sm[:, SM_HM] = float(par)
    sm[:, SM_EPS] = EPS
    sm[:, SM_NEG1] = -1.0
    sm[:, SM_PIH] = PIH
    sm[:, SM_TINY] = 1e-38
    d["smalls"] = sm
    offv = np.zeros(40, np.int32)
    for k in range(4):
        offv[k] = par * E + k * 512
    offv[16] = par * TL
    offv[17] = par * TL
    offv[25] = (1 - par) * TL
    d["offs"] = offv[None, :]
    dm = np.zeros((P, P), np.float32)
    dm[np.triu_indices(P, 1)] = NEG
    d["diagm"] = dm
    d["ident"] = np.eye(P, dtype=np.float32)
    d["cmatT"], d["u_lhs"], d["b_lhs"], d["a_lhs"] = CmatT, u_lhs, b_lhs, a_lhs
    gu = _spline_aux("gu", core, [("g", np.asarray(inputs["L0_ffn_gate_cp"])),
                                  ("u", np.asarray(inputs["L0_ffn_up_cp"]))])
    dn = _spline_aux("d", core, [("d", np.asarray(inputs["L0_ffn_down_cp"]))])
    d["gu_phim"], d["gu_u2"] = gu["phim"], gu["u2"]
    d["dn_phim"], d["dn_u2"] = dn["phim"], dn["u2"]
    d["cpw20_g"], d["cpw5_g"] = gu["cpw20g"], gu["cpw5g"]
    d["cpw20_u"], d["cpw5_u"] = gu["cpw20u"], gu["cpw5u"]
    d["cpw20_d"], d["cpw5_d"] = dn["cpw20d"], dn["cpw5d"]
    return d


_CACHE = {}


def kernel(**inputs):
    if "prog" not in _CACHE:
        _CACHE["prog"] = build_program()
    nc, _ = _CACHE["prog"]
    consts = _spline_const_tables()
    in_maps = [_host_inputs(inputs, c, consts) for c in range(8)]
    res = run_bass_kernel_spmd(nc, in_maps, core_ids=list(range(8)))
    out = np.empty((B, T, E), np.float32)
    for c in range(8):
        b, par = c // 2, c % 2
        out[b, par * TL:(par + 1) * TL, :] = res.results[c]["out_res"].T
    return out


# version 06 (final)
# speedup vs baseline: 1.0217x; 1.0217x over previous
import math
import numpy as np
import concourse.bass as bass
import concourse.mybir as mybir
from concourse.bass_utils import run_bass_kernel_spmd

F32 = mybir.dt.float32
BF16 = mybir.dt.bfloat16
I32 = mybir.dt.int32
AF = mybir.ActivationFunctionType
ALU = mybir.AluOpType
AX = mybir.AxisListType

B, T, E = 4, 2048, 2048
TL, P = 1024, 128
NF = E // P
INTER = 8192
N_CTRL = 130156
NSAMP = INTER * E
DT = (N_CTRL - 1.0) / (NSAMP - 1.0)
NCH_CORE = (NSAMP // 128) // 8
LAMBDA_INIT = 0.8 - 0.6 * math.exp(-0.3 * 12)
NEG = -1e30
SCALE = 1.0 / math.sqrt(128.0)
TWO_PI = 2.0 * math.pi
MAGIC = 12582912.0
EPS = 1e-5
PIH = math.pi / 2.0
RGALL = [[0, 1, 2, 3, 4, 5, 6, 7]]

SM_LAM, SM_G, SM_BIN, SM_BOUT = 0, 4, 6, 22
SM_BGX, SM_BGA, SM_AR, SM_AI = 38, 46, 54, 62
SM_HM, SM_NLAM, SM_GP, SM_SP, SM_N8SP = 70, 71, 72, 74, 82
SM_EPS, SM_NEG1, SM_PIH, SM_TINY = 90, 91, 92, 93

_C_POLY = np.array([
    [0.0, -0.5, 1.0, -0.5],
    [1.0, 0.0, -2.5, 1.5],
    [0.0, 0.5, 2.0, -1.5],
    [0.0, 0.0, -0.5, 0.5],
], dtype=np.float64)
_BCOEF = np.array([0.5, -1.0, 0.0, 1.0, -0.5], np.float64)
_ACOEF = np.array([0.5, -2.0, 3.0, -2.0, 0.5], np.float64)


def _spline_const_tables():
    from math import comb
    pdt = np.arange(128, dtype=np.float64) * DT
    Cmat = np.zeros((128, 20))
    for j in range(4):
        for m in range(4):
            v = np.zeros(128)
            for nn in range(m, 4):
                v += _C_POLY[j, nn] * comb(nn, m) * pdt ** (nn - m)
            Cmat[:, j * 4 + m] = v
    return (np.ascontiguousarray(Cmat.T).astype(np.float32),
            np.stack([pdt, np.ones(128)], 0).astype(np.float32),
            np.repeat(_BCOEF[:, None], 128, 1).astype(np.float32),
            np.repeat(_ACOEF[:, None], 128, 1).astype(np.float32))


def _spline_s0(kind, core):
    if kind == "gu":
        cols = []
        for hb in (2 * core, 2 * core + 1):
            base = hb * P
            cols.extend(i * E + base for i in range(INTER))
    else:
        cols = []
        for ht in (2 * core, 2 * core + 1):
            for it in range(64):
                for hh in range(P):
                    cols.append((ht * P + hh) * INTER + it * P)
    return np.asarray(cols, dtype=np.int64)


def _spline_aux(kind, core, cps):
    s0 = _spline_s0(kind, core)
    u0 = s0.astype(np.float64) * DT
    i_n = np.floor(u0).astype(np.int64)
    phi = u0 - i_n
    out = {"phim": np.tile(np.stack([phi ** m for m in range(4)], 0),
                           (5, 1)).astype(np.float32),
           "u2": np.stack([np.ones_like(phi), phi], 0).astype(np.float32)}
    for name, cp in cps:
        wj = np.stack([cp[np.clip(i_n - 1 + j, 0, N_CTRL - 1)]
                       for j in range(5)], 0).astype(np.float32)
        out["cpw5" + name] = np.ascontiguousarray(wj)
        out["cpw20" + name] = np.ascontiguousarray(np.repeat(wj, 4, axis=0))
    return out


class Buf:
    __slots__ = ("name", "writer", "readers")

    def __init__(self, name):
        self.name = name
        self.writer = None
        self.readers = []


class Sched:
    ENGINES = ("pe", "act", "dve", "pool", "sp")

    def __init__(self):
        self.ops = {e: [] for e in self.ENGINES}
        self.cnt = {"s_" + e: 0 for e in self.ENGINES}
        self.waited = {e: {} for e in self.ENGINES}
        self.pend = {e: [] for e in self.ENGINES}
        self.cc_id = 0

    def _dep(self, eng, deps, d, tag=""):
        if d is None:
            return
        sem, val = d
        if val <= self.waited[eng].get(sem, 0):
            return
        if deps.get(sem, 0) < val:
            deps[sem] = val

    def op(self, eng, fn, reads=(), writes=(), dma=False, commit=True, tag="",
           chan=None):
        if dma and eng in ("sp", "act"):
            assert chan, ("sp/act DMAs need a channel (HWDGE OOO)", tag)
        sem = ("c_" + chan) if (dma and chan) else ("s_" + eng)
        self.cnt.setdefault(sem, 0)
        deps = {}
        for b in reads:
            self._dep(eng, deps, b.writer, tag)
        for b in writes:
            self._dep(eng, deps, b.writer, tag)
            for d in b.readers:
                self._dep(eng, deps, d, tag)
        for s, v in deps.items():
            assert v <= self.cnt[s], (tag, s, v, self.cnt[s])
            if s.startswith("c_") and v < self.cnt[s]:
                print(f"WARNING partial chan wait {s} {v}<{self.cnt[s]} "
                      f"at {tag}")
            self.waited[eng][s] = max(self.waited[eng].get(s, 0), v)
        inc = (16 if dma else 1) if commit else None
        self.ops[eng].append((fn, deps, inc, sem, tag))
        if commit:
            self.cnt[sem] += inc
            here = (sem, self.cnt[sem])
            for b, isw in self.pend[eng]:
                if isw:
                    b.writer = here
                    b.readers = []
                else:
                    b.readers.append(here)
            self.pend[eng].clear()
            for b in writes:
                b.writer = here
                b.readers = []
            for b in reads:
                b.readers.append(here)
        else:
            for b in writes:
                self.pend[eng].append((b, True))
            for b in reads:
                self.pend[eng].append((b, False))

    def collective(self, fn, reads=(), writes=(), tag="cc"):
        name = f"s_cc{self.cc_id}"
        self.cc_id += 1
        self.cnt[name] = 1
        deps = {}
        for b in reads:
            self._dep("pool", deps, b.writer)
        for s, v in deps.items():
            self.waited["pool"][s] = max(self.waited["pool"].get(s, 0), v)
        self.ops["pool"].append((fn, deps, ("cc", name), None, tag))
        for b in writes:
            b.writer = (name, 1)
            b.readers = []

    def settle(self, bufs, chan):
        sem = "c_" + chan
        for b in bufs:
            b.writer = (sem, self.cnt[sem])

    def claim(self, newbuf, oldbufs):
        newbuf.writer = None
        rd = []
        for ob in oldbufs:
            if ob.writer:
                rd.append(ob.writer)
            rd.extend(ob.readers)
        newbuf.readers = rd

    def build(self, nc, es, block):
        sems = {}
        for name in sorted(self.cnt):
            assert self.cnt[name] < 60000, (name, self.cnt[name])
            sems[name] = es.enter_context(nc.semaphore(name))

        def mk_body(engname):
            oplist = self.ops[engname]

            def body(eng):
                for fn, deps, inc, opsem, tag in oplist:
                    for s in sorted(deps):
                        eng.wait_ge(sems[s], deps[s])
                    ins = fn(eng)
                    if inc is None:
                        continue
                    if isinstance(inc, tuple):
                        ins.then_inc(sems[inc[1]])
                    else:
                        ins.then_inc(sems[opsem], inc)
            return body

        for e, attr in (("sp", "sync"), ("pe", "tensor"), ("act", "scalar"),
                        ("dve", "vector"), ("pool", "gpsimd")):
            if self.ops[e]:
                getattr(block, attr)(mk_body(e))


def build_program(taps=()):
    import contextlib
    nc = bass.Bass("TRN2", target_bir_lowering=False)
    nc.disable_value_cache = True
    S = Sched()
    es = contextlib.ExitStack()
    taps = set(taps)

    DI = lambda n, s, d=F32: nc.dram_tensor(n, list(s), d, kind="ExternalInput")
    DO = lambda n, s, d=F32: nc.dram_tensor(n, list(s), d, kind="ExternalOutput")

    x_lo = DI("x_lo", (E, TL))
    x_hi = DI("x_hi", (E, TL))
    x_res = DI("x_res", (E, TL))
    wq_d = DI("wq", (E, 1024))
    wk_d = DI("wk", (E, 1024))
    wv_d = DI("wv", (E, 1024))
    wo_d = DI("wo", (1024, E))
    win_d = DI("win", (E, E))
    wout_d = DI("wout", (E, E))
    wgx_d = DI("wgx", (16 * P, P))
    wga_d = DI("wga", (16 * P, P))
    smalls_d = DI("smalls", (P, 96))
    offs_d = DI("offs", (1, 40), I32)
    diagm_d = DI("diagm", (P, P))
    ident_d = DI("ident", (P, P))
    cm_d = DI("cmatT", (20, P))
    ul_d = DI("u_lhs", (2, P))
    bl_d = DI("b_lhs", (5, P))
    al_d = DI("a_lhs", (5, P))
    gu_phim = DI("gu_phim", (20, NCH_CORE))
    gu_u2 = DI("gu_u2", (2, NCH_CORE))
    dn_phim = DI("dn_phim", (20, NCH_CORE))
    dn_u2 = DI("dn_u2", (2, NCH_CORE))
    cpw20 = {w: DI("cpw20_" + w, (20, NCH_CORE)) for w in ("g", "u", "d")}
    cpw5 = {w: DI("cpw5_" + w, (5, NCH_CORE)) for w in ("g", "u", "d")}

    out_res = DO("out_res", (E, TL))
    tap_outs = {}

    def tapout(name, shape, dt=F32):
        tap_outs[name] = DO("tap_" + name, shape, dt)
        return tap_outs[name]

    wshard = nc.dram_tensor("wshard", [768, INTER], BF16)
    wg = nc.dram_tensor("wg", [6144, INTER], BF16, addr_space="Shared")
    qt_dram = nc.dram_tensor("qt_dram", [16, 8, P, P], BF16)
    pdram = nc.dram_tensor("pdram", [2 * TL, TL], BF16)
    xch_a = nc.dram_tensor("xch_a", [2 * E, T], F32, addr_space="Shared")
    xch_b = nc.dram_tensor("xch_b", [2 * TL, 2], F32, addr_space="Shared")
    parta = nc.dram_tensor("parta", [E, T], F32)
    partb = nc.dram_tensor("partb", [2 * E, TL], F32)
    partc = nc.dram_tensor("partc", [TL, 2], F32)
    partd = nc.dram_tensor("partd", [TL, 2], F32)
    bar_in = nc.dram_tensor("bar_in", [1, 8], F32)
    bar_out = [nc.dram_tensor(f"bar_out{i}", [8, 8], F32) for i in range(3)]

    res_t = es.enter_context(nc.sbuf_tensor("res", [P, NF * TL], F32))
    arena = es.enter_context(nc.sbuf_tensor("arena", [P, 61440], BF16))
    onesc = es.enter_context(nc.sbuf_tensor("onesc", [P, 1], F32))
    onescb = es.enter_context(nc.sbuf_tensor("onescb", [P, 1], BF16))
    onesr = es.enter_context(nc.sbuf_tensor("onesr", [1, P], F32))
    ones256 = es.enter_context(nc.sbuf_tensor("ones256", [P, 256], F32))
    identb = es.enter_context(nc.sbuf_tensor("ident_s", [P, P], BF16))
    diagm = es.enter_context(nc.sbuf_tensor("diagm_s", [P, P], F32))
    smalls = es.enter_context(nc.sbuf_tensor("smalls_s", [P, 96], F32))
    rows = es.enter_context(nc.sbuf_tensor("rows", [1, 1536], F32))
    dmp = es.enter_context(nc.sbuf_tensor("dmp", [1, 16], F32))
    dbgs = es.enter_context(nc.sbuf_tensor("dbgs", [P, 2048], F32))
    cols = es.enter_context(nc.sbuf_tensor("cols", [P, 16], F32))
    carry = es.enter_context(nc.sbuf_tensor("carry", [P, 32], F32))
    hend = es.enter_context(nc.sbuf_tensor("hend", [P, 16], F32))
    hin = es.enter_context(nc.sbuf_tensor("hin", [P, 16], F32))
    offs_s = es.enter_context(nc.sbuf_tensor("offs_s", [1, 40], I32))
    cm_t = es.enter_context(nc.sbuf_tensor("cm_t", [20, P], F32))
    ul_t = es.enter_context(nc.sbuf_tensor("ul_t", [2, P], F32))
    bl_t = es.enter_context(nc.sbuf_tensor("bl_t", [5, P], F32))
    al_t = es.enter_context(nc.sbuf_tensor("al_t", [5, P], F32))

    ps1 = es.enter_context(nc.psum_tensor("ps1", [P, 2048], F32))
    ps2 = es.enter_context(nc.psum_tensor("ps2", [P, 1024], F32))
    ps3 = es.enter_context(nc.psum_tensor("ps3", [P, 512], F32))
    psT = es.enter_context(nc.psum_tensor("psT", [P, 1024], BF16))

    b_res = Buf("res")
    b_ps1 = [Buf(f"ps1_{i}") for i in range(4)]
    b_ps2 = [Buf(f"ps2_{i}") for i in range(2)]
    b_ps3 = Buf("ps3")
    b_psT = Buf("psT")
    b_rows, b_cols, b_carry = Buf("rows"), Buf("cols"), Buf("carry")
    b_hend, b_hin = Buf("hend"), Buf("hin")
    b_smalls, b_offs = Buf("smalls"), Buf("offs")
    b_ones, b_ident, b_diagm, b_rct = (Buf("ones"), Buf("ident"),
                                       Buf("diagm"), Buf("rct"))
    b_wshard, b_wg, b_pdram = Buf("wshard"), Buf("wg"), Buf("pdram")
    b_qtd = Buf("qt_dram")
    b_xch_a, b_xch_b = Buf("xch_a"), Buf("xch_b")
    b_parta, b_partb = Buf("parta"), Buf("partb")
    b_partc, b_partd = Buf("partc"), Buf("partd")

    hold = {}

    def dma(q, out_ap, in_ap, reads=(), writes=(), tag="", chan=None):
        S.op(q, lambda e, o=out_ap, i=in_ap: e.dma_start(out=o, in_=i),
             reads=reads, writes=writes, dma=True, tag=tag, chan=chan)

    def mm(out_ap, lhsT, rhs, start, stop, reads=(), writes=(), tag="",
           commit=None):
        S.op("pe",
             lambda e, o=out_ap, l=lhsT, r=rhs, a=start, b=stop:
             e.matmul(o, l, r, start=a, stop=b),
             reads=reads, writes=writes,
             commit=(stop if commit is None else commit), tag=tag)

    def trans(out_ap, in_ap, reads=(), writes=(), tag=""):
        S.op("pe", lambda e, o=out_ap, i=in_ap: e.transpose(o, i, identb[:, :]),
             reads=list(reads) + [b_ident], writes=writes, tag=tag)

    def act(out_ap, in_ap, func, bias=0.0, scale=1.0, accum=None,
            reads=(), writes=(), tag=""):
        S.op("act",
             lambda e, o=out_ap, i=in_ap, f=func, bi=bias, sc=scale, ac=accum:
             e.activation(o, i, f, bias=bi, scale=sc, accum_out=ac),
             reads=reads, writes=writes, tag=tag)

    def dve(meth, *aps, reads=(), writes=(), tag="", eng="dve"):
        S.op("dve", lambda e, m=meth, a=aps: getattr(e, m)(*a),
             reads=reads, writes=writes, tag=tag)

    def abf(off_kb, cols_):
        o = int(off_kb * 512)
        return arena[:, o:o + cols_]

    def af32(off_kb, cols_):
        o = int(off_kb * 512)
        return arena[:, o:o + 2 * cols_].bitcast(F32)

    dve("memset", onesc[:, :], 1.0, writes=[b_ones], tag="o1")
    dve("memset", onescb[:, :], 1.0, writes=[b_ones], tag="o1b")
    dve("memset", onesr[:, :], 1.0, writes=[b_ones], tag="o2")
    dve("memset", ones256[:, :], 1.0, writes=[b_ones], tag="o3")
    dve("memset", carry[:, :], 0.0, writes=[b_carry], tag="o4")
    dma("pool", identb[:, :], ident_d[:, :], writes=[b_ident], tag="lid",
        chan="lid")
    dma("sp", diagm[:, :], diagm_d[:, :], writes=[b_diagm], tag="ldg",
        chan="ldg")
    dma("act", smalls[:, :], smalls_d[:, :], writes=[b_smalls], tag="lsm",
        chan="sml")
    dma("sp", offs_s[:, :], offs_d[:, :], writes=[b_offs], tag="lof",
        chan="lof")
    for t_, d_ in ((cm_t, cm_d), (ul_t, ul_d), (bl_t, bl_d), (al_t, al_d)):
        dma("sp", t_[:, :], d_[:, :], writes=[b_rct], tag="lrc", chan="lrc")

    def dyn_dma(idx, mx, out_fn, in_ap, reads=(), writes=(), tag=""):
        def f(e, idx=idx, mx=mx, out_fn=out_fn, in_ap=in_ap):
            r = e.alloc_register(None)
            e.reg_load(r, offs_s[0:1, idx:idx + 1])
            off = e.snap(r, min_val=0, max_val=mx)
            ins = e.dma_start(out=out_fn(off), in_=in_ap)
            e.free_register(r)
            return ins
        S.op("pool", f, reads=list(reads) + [b_offs], writes=writes, dma=True,
             tag=tag)

    def dyn_dma_in(idx, mx, out_ap, in_fn, reads=(), writes=(), tag=""):
        def f(e, idx=idx, mx=mx, out_ap=out_ap, in_fn=in_fn):
            r = e.alloc_register(None)
            e.reg_load(r, offs_s[0:1, idx:idx + 1])
            off = e.snap(r, min_val=0, max_val=mx)
            ins = e.dma_start(out=out_ap, in_=in_fn(off))
            e.free_register(r)
            return ins
        S.op("pool", f, reads=list(reads) + [b_offs], writes=writes, dma=True,
             tag=tag)

    for f in range(NF):
        dma("sp", res_t[:, f * TL:(f + 1) * TL], x_res[f * P:(f + 1) * P, :],
            writes=[b_res], tag="lre", chan="lre")

    dve("tensor_tensor", cols[:, 0:1], smalls[:, SM_LAM:SM_LAM + 1],
        smalls[:, SM_LAM + 1:SM_LAM + 2], ALU.mult,
        reads=[b_smalls], writes=[b_cols], tag="lm1")
    dve("tensor_tensor", cols[:, 1:2], smalls[:, SM_LAM + 2:SM_LAM + 3],
        smalls[:, SM_LAM + 3:SM_LAM + 4], ALU.mult,
        reads=[b_smalls], writes=[b_cols], tag="lm2")
    mm(ps3[0:1, 0:2], onesc[:, :], cols[:, 0:2], True, True,
       reads=[b_cols, b_ones], writes=[b_ps3], tag="lmd")
    act(rows[0:1, 0:2], ps3[0:1, 0:2], AF.Exp, reads=[b_ps3], writes=[b_rows],
        tag="lme")
    dve("tensor_tensor", rows[0:1, 2:3], rows[0:1, 0:1], rows[0:1, 1:2],
        ALU.subtract, reads=[b_rows], writes=[b_rows], tag="lms")
    dve("tensor_scalar", rows[0:1, 3:4], rows[0:1, 2:3], -1.0, -LAMBDA_INIT,
        ALU.mult, ALU.add, reads=[b_rows], writes=[b_rows], tag="lmn")
    b_dmp = Buf("dmp")
    b_dbgs = Buf("dbgs")
    if "dbg" in taps:
        dve("tensor_copy", dmp[0:1, 6:10], smalls[0:1, 0:4],
            reads=[b_smalls], writes=[b_dmp], tag="dcp0")
        dve("tensor_copy", dmp[0:1, 10:12], cols[0:1, 0:2],
            reads=[b_cols], writes=[b_dmp], tag="dcp0b")
        dve("tensor_copy", dmp[0:1, 0:2], ps3[0:1, 0:2],
            reads=[b_ps3], writes=[b_dmp], tag="dcp1")
    mm(ps3[:, 0:1], onesr[:, :], rows[0:1, 3:4], True, True,
       reads=[b_rows, b_ones], writes=[b_ps3], tag="lmb")
    dve("tensor_copy", smalls[:, SM_NLAM:SM_NLAM + 1], ps3[:, 0:1],
        reads=[b_ps3], writes=[b_smalls], tag="lmc")
    if "dbg" in taps:
        dve("tensor_copy", dmp[0:1, 2:6], rows[0:1, 0:4],
            reads=[b_rows], writes=[b_dmp], tag="dcp2")
    dve("tensor_scalar", smalls[:, SM_GP:SM_GP + 2], smalls[:, SM_G:SM_G + 2],
        1.0 - LAMBDA_INIT, None, ALU.mult,
        reads=[b_smalls], writes=[b_smalls], tag="gp")
    dve("tensor_scalar", cols[:, 0:8], smalls[:, SM_AR:SM_AR + 8], -1.0, None,
        ALU.mult, reads=[b_smalls], writes=[b_cols], tag="sp1")
    dve("tensor_tensor", cols[:, 8:16], cols[:, 0:8],
        smalls[:, SM_AR:SM_AR + 8], ALU.min,
        reads=[b_cols, b_smalls], writes=[b_cols], tag="sp2")
    act(cols[:, 8:16], cols[:, 8:16], AF.Exp, reads=[b_cols], writes=[b_cols],
        tag="sp3")
    act(cols[:, 8:16], cols[:, 8:16], AF.Ln, bias=1.0,
        reads=[b_cols], writes=[b_cols], tag="sp4")
    dve("tensor_scalar_max", cols[:, 0:8], smalls[:, SM_AR:SM_AR + 8], 0.0,
        reads=[b_smalls], writes=[b_cols], tag="sp5")
    dve("tensor_tensor", smalls[:, SM_SP:SM_SP + 8], cols[:, 0:8],
        cols[:, 8:16], ALU.add, reads=[b_cols], writes=[b_smalls], tag="sp6")
    dve("tensor_scalar", smalls[:, SM_N8SP:SM_N8SP + 8],
        smalls[:, SM_SP:SM_SP + 8], -8.0, None, ALU.mult,
        reads=[b_smalls], writes=[b_smalls], tag="sp7")

    def emit_ln_chunk(dst_ap_fn, dst_bufs, tag, sq_off, scr_parents):
        sq = abf(sq_off, 512)
        bc1 = abf(sq_off + 1, 512)
        bc2 = abf(sq_off + 2, 512)
        b_sq = Buf(tag + "sq")
        b_bc = Buf(tag + "bc")
        S.claim(b_sq, scr_parents)
        S.claim(b_bc, scr_parents)
        for f in range(NF):
            act(sq[:, :], dst_ap_fn(f), AF.Square,
                reads=[dst_bufs[f]], writes=[b_sq], tag=tag + "s")
            mm(ps3[0:1, 0:512], onescb[:, :], dst_ap_fn(f),
               start=(f == 0), stop=(f == NF - 1),
               reads=[dst_bufs[f], b_ones], writes=[b_ps3], tag=tag + "m1")
            mm(ps2[0:1, 0:512], onescb[:, :], sq[:, :],
               start=(f == 0), stop=(f == NF - 1),
               reads=[b_sq, b_ones], writes=[b_ps2[0]], tag=tag + "m2")
        dve("tensor_scalar", rows[0:1, 0:512], ps3[0:1, 0:512], 1.0 / E, None,
            ALU.mult, reads=[b_ps3], writes=[b_rows], tag=tag + "mn")
        dve("tensor_tensor", rows[0:1, 512:1024], rows[0:1, 0:512],
            rows[0:1, 0:512], ALU.mult, reads=[b_rows], writes=[b_rows],
            tag=tag + "m2r")
        dve("scalar_tensor_tensor", rows[0:1, 512:1024], ps2[0:1, 0:512],
            1.0 / E, rows[0:1, 512:1024], ALU.mult, ALU.subtract,
            reads=[b_ps2[0], b_rows], writes=[b_rows], tag=tag + "v")
        act(rows[0:1, 512:1024], rows[0:1, 512:1024], AF.Ln,
            bias=smalls[0:1, SM_EPS:SM_EPS + 1],
            reads=[b_rows], writes=[b_rows], tag=tag + "l")
        act(rows[0:1, 512:1024], rows[0:1, 512:1024], AF.Exp, scale=-0.5,
            reads=[b_rows], writes=[b_rows], tag=tag + "e")
        dve("tensor_tensor", rows[0:1, 1024:1536], rows[0:1, 0:512],
            rows[0:1, 512:1024], ALU.mult, reads=[b_rows], writes=[b_rows],
            tag=tag + "mr")
        mm(ps3[:, 0:512], onesr[:, :], rows[0:1, 512:1024], True, True,
           reads=[b_rows, b_ones], writes=[b_ps3], tag=tag + "b1")
        mm(ps2[:, 0:512], onesr[:, :], rows[0:1, 1024:1536], True, True,
           reads=[b_rows, b_ones], writes=[b_ps2[0]], tag=tag + "b2")
        dve("tensor_copy", bc1[:, :], ps3[:, 0:512], reads=[b_ps3],
            writes=[b_bc], tag=tag + "x1")
        dve("tensor_copy", bc2[:, :], ps2[:, 0:512], reads=[b_ps2[0]],
            writes=[b_bc], tag=tag + "x2")
        for f in range(NF):
            eng = "dve" if f % 2 == 0 else "pool"
            dve("tensor_tensor", dst_ap_fn(f), dst_ap_fn(f), bc1[:, :],
                ALU.mult, reads=[b_bc], writes=[dst_bufs[f]], tag=tag + "a1",
                eng=eng)
            dve("tensor_tensor", dst_ap_fn(f), dst_ap_fn(f), bc2[:, :],
                ALU.subtract, reads=[b_bc], writes=[dst_bufs[f]],
                tag=tag + "a2", eng=eng)
        return [b_sq, b_bc]

    AK_XNC, AK_WS, AK_KT, AK_V, AK_QST = 0, 16, 24, 56, 88
    xnc_ap = lambda f: abf(AK_XNC + f, 512)
    ws_ap = abf(AK_WS, 4096)
    kt_ap = lambda i: abf(AK_KT + 4 * i, 2048)
    v_ap = lambda tt: abf(AK_V + 2 * tt, 1024)
    qst_ap = abf(AK_QST, 512)

    b_kt, b_v = Buf("kt"), Buf("v")
    b_ws = Buf("ws")
    b_qst = Buf("qst")
    stag = [af32(92, 2048), af32(100, 2048)]
    stagv = af32(92, 4096)
    ws2_ap = abf(108, 4096)
    ws_slots = [ws_ap, ws2_ap]
    b_stag = [Buf("stag0"), Buf("stag1")]
    b_wsl2 = [b_ws, Buf("ws2")]
    wstg_ctr = [0]
    xstag = [af32(116, 512), af32(118, 512)]
    b_xstag = [Buf("xstag0"), Buf("xstag1")]
    prev_xt = None
    ln_scr = [Buf("z0")]
    for tch in range(4):
        xsrc = x_lo if tch < 2 else x_hi
        c0 = (tch % 2) * 512
        b_xt = [Buf(f"xt{tch}_{f}") for f in range(NF)]
        if prev_xt is not None:
            for f in range(NF):
                S.claim(b_xt[f], [prev_xt[f]])
        prev_xt = b_xt
        for f in range(NF):
            xp = f % 2
            b_xs = Buf(f"xs{tch}_{f}")
            S.claim(b_xs, [b_xstag[xp]])
            b_xstag[xp] = b_xs
            dma("sp" if xp == 0 else "act", xstag[xp][:, :],
                xsrc[f * P:(f + 1) * P, c0:c0 + 512],
                writes=[b_xs], tag="lx", chan=f"lx{xp}")
            dve("tensor_copy", xnc_ap(f), xstag[xp][:, :],
                reads=[b_xs], writes=[b_xt[f]], tag="xcast")
        ln_scr = emit_ln_chunk(xnc_ap, b_xt, f"l{tch}", AK_QST + 1, ln_scr)
        if "xn" in taps:
            if "xn" not in tap_outs:
                tapout("xn", (E, T))
            for f in range(NF):
                dma("pool", tap_outs["xn"][f * P:(f + 1) * P,
                                           tch * 512:(tch + 1) * 512],
                    xnc_ap(f), reads=[b_xt[f]], tag="txn")
        for wdram, wtag in ((wq_d, "q"), (wk_d, "k")):
            for qc in range(8):
                p = wstg_ctr[0] % 2
                wstg_ctr[0] += 1
                b_sn = Buf(f"sg{wtag}{tch}{qc}")
                S.claim(b_sn, [b_stag[p]])
                b_stag[p] = b_sn
                for kk in range(4):
                    dma("sp" if p == 0 else "act",
                        stag[p][:, kk * 512:(kk + 1) * 512].rearrange(
                            "p (k c) -> p k c", k=4),
                        wdram[kk * 512:(kk + 1) * 512,
                              qc * P:(qc + 1) * P].rearrange(
                                  "(k p) c -> p k c", p=P),
                        writes=[b_sn], tag=f"lw{wtag}", chan=f"wq{p}")
                b_wn = Buf(f"w{wtag}{tch}{qc}")
                S.claim(b_wn, [b_wsl2[p]])
                b_wsl2[p] = b_wn
                dve("tensor_copy", ws_slots[p][:, 0:2048], stag[p][:, :],
                    reads=[b_sn], writes=[b_wn], tag="wcast")
                pslice = ps1[:, (qc % 2) * 512:(qc % 2) * 512 + 512]
                pbuf = b_ps1[qc % 2]
                for ek in range(16):
                    mm(pslice, ws_slots[p][:, ek * P:(ek + 1) * P],
                       xnc_ap(ek), start=(ek == 0), stop=(ek == 15),
                       reads=[b_wn, b_xt[ek]], writes=[pbuf], tag=f"p{wtag}")
                if wtag == "k":
                    act(kt_ap(qc)[:, tch * 512:tch * 512 + 512], pslice,
                        AF.Copy, reads=[pbuf], writes=[b_kt], tag="ek")
                else:
                    b_qn = Buf(f"qs{tch}{qc}")
                    S.claim(b_qn, [b_qst])
                    b_qst = b_qn
                    act(qst_ap[:, :], pslice, AF.Copy,
                        reads=[pbuf], writes=[b_qst], tag="eq")
                    for qbi in range(4):
                        qb = tch * 4 + qbi
                        dma("sp", qt_dram[qb, qc, :, :],
                            qst_ap[:, qbi * P:(qbi + 1) * P],
                            reads=[b_qst], writes=[b_qtd], tag="sq",
                            chan="sq")
        for ec2 in range(4):
            b_sn = Buf(f"sv{tch}{ec2}")
            S.claim(b_sn, [b_stag[0], b_stag[1]])
            b_stag[0] = b_stag[1] = b_sn
            for kk in range(4):
                dma("sp" if kk % 2 == 0 else "act",
                    stagv[:, kk * 1024:kk * 1024 + 1024].rearrange(
                        "p (k c) -> p k c", k=4),
                    wv_d[kk * 512:(kk + 1) * 512,
                         ec2 * 256:(ec2 + 1) * 256].rearrange(
                             "(k p) c -> p k c", p=P),
                    writes=[b_sn], tag="lwv", chan="wv")
            pv = ec2 % 2
            b_wn = Buf(f"wv{tch}{ec2}")
            S.claim(b_wn, [b_wsl2[pv]])
            b_wsl2[pv] = b_wn
            dve("tensor_copy", ws_slots[pv][:, 0:4096], stagv[:, :],
                reads=[b_sn], writes=[b_wn], tag="vcast")
            for ts4 in range(4):
                tt = tch * 4 + ts4
                pslice = ps2[:, (ts4 % 2) * 512:(ts4 % 2) * 512 + 256]
                pbuf = b_ps2[ts4 % 2]
                for ek in range(16):
                    mm(pslice, xnc_ap(ek)[:, ts4 * P:(ts4 + 1) * P],
                       ws_slots[pv][:, ek * 256:(ek + 1) * 256],
                       start=(ek == 0), stop=(ek == 15),
                       reads=[b_wn, b_xt[ek]], writes=[pbuf], tag="pv")
                act(v_ap(tt)[:, ec2 * 256:(ec2 + 1) * 256], pslice, AF.Copy,
                    reads=[pbuf], writes=[b_v], tag="ev")

    qkv_scrap = [b_stag[0], b_stag[1], b_wsl2[0], b_wsl2[1]]
    b_ws = b_wsl2[0]
    if "kt" in taps:
        tp = tapout("kt", (1024, T))
        for i in range(8):
            dma("pool", tp[i * P:(i + 1) * P, :], kt_ap(i), reads=[b_kt],
                tag="tkt")
    if "vv" in taps:
        tp = tapout("vv", (T, 1024))
        for tt in range(16):
            dma("pool", tp[tt * P:(tt + 1) * P, :], v_ap(tt), reads=[b_v],
                tag="tvv")
    if "qt" in taps:
        tp = tapout("qt", (16 * 8 * P, P), BF16)
        dma("pool", tp[:, :], qt_dram[:, :, :, :].rearrange(
            "a b p q -> (a b p) q"), reads=[b_qtd], tag="tqt")

    R0 = 92

    def emit_recon():
        r20a = [af32(R0 + 2 * i, 512)[0:20, :] for i in range(2)]
        rpha = [af32(R0 + 4 + 2 * i, 512)[0:20, :] for i in range(2)]
        ru2a = [af32(R0 + 8 + 2 * i, 512)[0:2, :] for i in range(2)]
        rc5a = [af32(R0 + 12 + 2 * i, 512)[0:5, :] for i in range(2)]
        rxa = af32(R0 + 16, 512)[0:20, :]
        rva = af32(R0 + 18, 1024)
        raba = af32(R0 + 22, 1024)
        rwa = abf(R0 + 26, 1024)
        b_r20 = [Buf(f"r20{i}") for i in range(2)]
        b_rph = [Buf(f"rph{i}") for i in range(2)]
        b_ru2 = [Buf(f"ru2{i}") for i in range(2)]
        b_rc5 = [Buf(f"rc5{i}") for i in range(2)]
        b_rx, b_rv, b_rab, b_rw = Buf("rx"), Buf("rv"), Buf("rab"), Buf("rw")
        for _rb in (b_r20 + b_rph + b_ru2 + b_rc5 + [b_rx, b_rv, b_rab, b_rw]):
            S.claim(_rb, qkv_scrap)
        wrows = {"g": 0, "u": 256, "d": 512}
        for wname in ("g", "u", "d"):
            phim_ = gu_phim if wname in ("g", "u") else dn_phim
            u2_ = gu_u2 if wname in ("g", "u") else dn_u2
            for ch in range(NCH_CORE // 512):
                sl = ch % 2
                c0 = ch * 512
                dma("sp", r20a[sl][:, :], cpw20[wname][:, c0:c0 + 512],
                    writes=[b_r20[sl]], tag="r20", chan=f"r2{sl}")
                dma("sp", rpha[sl][:, :], phim_[:, c0:c0 + 512],
                    writes=[b_rph[sl]], tag="rph", chan=f"rp{sl}")
                dma("act", ru2a[sl][:, :], u2_[:, c0:c0 + 512],
                    writes=[b_ru2[sl]], tag="ru2", chan=f"rq{sl}")
                dma("act", rc5a[sl][:, :], cpw5[wname][:, c0:c0 + 512],
                    writes=[b_rc5[sl]], tag="rc5", chan=f"rf{sl}")
                dve("tensor_tensor", rxa[:, :], r20a[sl][:, :],
                    rpha[sl][:, :], ALU.mult,
                    reads=[b_r20[sl], b_rph[sl]], writes=[b_rx], tag="rx")
                mm(ps1[:, 1024:1536], cm_t[:, :], rxa[:, :], True, True,
                   reads=[b_rx, b_rct], writes=[b_ps1[2]], tag="rwa")
                mm(ps1[:, 1536:2048], ul_t[:, :], ru2a[sl][:, :], True, True,
                   reads=[b_ru2[sl], b_rct], writes=[b_ps1[3]], tag="ru")
                mm(ps2[:, 512:1024], bl_t[:, :], rc5a[sl][:, :], True, True,
                   reads=[b_rc5[sl], b_rct], writes=[b_ps2[1]], tag="rbb")
                mm(psT[:, 0:1024].bitcast(F32)[:, 0:256], al_t[:, :],
                   rc5a[sl][:, 0:256], True, True,
                   reads=[b_rc5[sl], b_rct], writes=[b_psT], tag="ra1")
                mm(psT[:, 0:1024].bitcast(F32)[:, 256:512], al_t[:, :],
                   rc5a[sl][:, 256:512], True, True,
                   reads=[b_rc5[sl], b_rct], writes=[b_psT], tag="ra2")
                act(rva[:, 0:512], ps1[:, 1536:2048], AF.Relu,
                    bias=smalls[:, SM_NEG1:SM_NEG1 + 1],
                    reads=[b_ps1[3], b_smalls], writes=[b_rv], tag="rv")
                act(rva[:, 512:1024], rva[:, 0:512], AF.Square,
                    reads=[b_rv], writes=[b_rv], tag="rv2")
                act(rwa[:, 0:512], ps1[:, 1024:1536], AF.Copy,
                    reads=[b_ps1[2]], writes=[b_rw], tag="rwe")
                dve("tensor_tensor", raba[:, 0:512], rva[:, 0:512],
                    psT[:, 0:1024].bitcast(F32)[:, 0:512], ALU.mult,
                    reads=[b_rv, b_psT], writes=[b_rab], tag="rm1")
                dve("tensor_tensor", raba[:, 512:1024], raba[:, 0:512],
                    ps2[:, 512:1024], ALU.add, reads=[b_rab, b_ps2[1]],
                    writes=[b_rab], tag="rm2")
                dve("tensor_tensor", raba[:, 0:512], rva[:, 512:1024],
                    raba[:, 512:1024], ALU.mult, reads=[b_rv, b_rab],
                    writes=[b_rab], tag="rm3", eng="pool")
                dve("tensor_tensor", rwa[:, 512:1024], raba[:, 0:512],
                    rwa[:, 0:512], ALU.add, reads=[b_rab, b_rw],
                    writes=[b_rw], tag="rw", eng="pool")
                if wname in ("g", "u"):
                    hb_l, i0 = divmod(c0, INTER)
                    dst = wshard[wrows[wname] + hb_l * P:
                                 wrows[wname] + hb_l * P + P, i0:i0 + 512]
                else:
                    ht_l, i0 = divmod(c0, INTER)
                    dst = wshard[512 + ht_l * P:512 + ht_l * P + P,
                                 i0:i0 + 512]
                dma("sp", dst, rwa[:, 512:1024], reads=[b_rw],
                    writes=[b_wshard], tag="rwo", chan="rwo")

        def cc_w(e):
            return e.collective_compute(
                "AllGather", ALU.bypass, replica_groups=RGALL,
                ins=[wshard.ap().opt()], outs=[wg.ap().opt()])
        S.collective(cc_w, reads=[b_wshard], writes=[b_wg], tag="agw")
        return ([b_r20[0], b_r20[1], b_rph[0], b_rph[1], b_ru2[0], b_ru2[1],
                 b_rc5[0], b_rc5[1], b_rx, b_rv, b_rab, b_rw])
    recon_bufs = emit_recon()
    if "wshard" in taps:
        tp = tapout("wshard", (768, INTER), BF16)
        tb = Buf("tws")
        sl = abf(R0, 2048)
        S.claim(tb, recon_bufs)
        for r in range(0, 768, P):
            for cc0 in range(0, INTER, 2048):
                dma("pool", sl[:, 0:2048], wshard[r:r + P, cc0:cc0 + 2048],
                    reads=[b_wshard], writes=[tb], tag="tw1")
                dma("pool", tp[r:r + P, cc0:cc0 + 2048], sl[:, 0:2048],
                    reads=[tb], tag="tw2")
        recon_bufs = recon_bufs + [tb]

    if "dbg" in taps:
        tapout("dbg_lam", (P, 20))
        dma("pool", tap_outs["dbg_lam"][:, 0:4], smalls[:, 70:74],
            reads=[b_smalls], tag="tlam")
        dma("pool", tap_outs["dbg_lam"][0:1, 4:16], dmp[0:1, 0:12],
            reads=[b_dmp], tag="tlam2")
        tapout("dbg_z", (P, 8))
        tapout("dbg_e", (P, 768), BF16)
        tapout("dbg_at", (P, 768), BF16)
        tapout("dbg_og", (8 * P, 512), BF16)

    def emit_attn():
        wos_ap = abf(0, 4096)
        wos_slots = [Buf("wos0"), Buf("wos1")]
        S.claim(wos_slots[0], prev_xt + [b_ws] + ln_scr)
        S.claim(wos_slots[1], prev_xt + [b_ws] + ln_scr)
        wostag = [af32(8, 1024), af32(12, 1024)]
        b_wostag = [Buf("wostag0"), Buf("wostag1")]
        S.claim(b_wostag[0], prev_xt + [b_ws] + ln_scr)
        S.claim(b_wostag[1], prev_xt + [b_ws] + ln_scr)
        qtb_ap = abf(88, 1024)
        b_qtb = Buf("qtb")
        S.claim(b_qtb, [b_qst])
        e_ap = lambda cc: abf(90 + 4 * cc, 2048)
        b_e = Buf("e12")
        S.claim(b_e, recon_bufs)
        at_ap = abf(98, 2048)
        b_at = Buf("at")
        S.claim(b_at, recon_bufs)
        og_ap = lambda ok: abf(102 + ok, 512)
        b_og = Buf("og")
        S.claim(b_og, recon_bufs)
        osq_ap = abf(110, 256)
        rb_ap = abf(110.5, 128)
        b_osq, b_rb = Buf("osq"), Buf("rb")
        S.claim(b_osq, recon_bufs)
        S.claim(b_rb, recon_bufs)
        stg_ap = af32(112, 1024)
        b_stg = Buf("stg")
        S.claim(b_stg, recon_bufs)
        wop_ps = [ps1[:, 0:512], ps1[:, 512:1024], ps1[:, 1024:1536],
                  ps1[:, 1536:2048], ps2[:, 0:512], ps2[:, 512:1024],
                  ps3[:, 0:512], psT[:, 0:1024].bitcast(F32)[:, 0:512]]
        wop_pb = [b_ps1[0], b_ps1[1], b_ps1[2], b_ps1[3], b_ps2[0], b_ps2[1],
                  b_ps3, b_psT]

        for qb in range(16):
            dma("sp", qtb_ap[:, :].rearrange("p (b q) -> p b q", b=8),
                qt_dram[qb, :, :, :].rearrange("b p q -> p b q"),
                reads=[b_qtd], writes=[b_qtb], tag="lqt", chan="lqt")
            for h in range(4):
                ext = (qb + 1) * P
                nkc = (ext + 511) // 512
                for cc in range(2):
                    qci = h * 2 + cc
                    for kc in range(nkc):
                        w = min(512, ext - kc * 512)
                        mm(ps1[:, kc * 512:kc * 512 + w],
                           qtb_ap[:, qci * P:(qci + 1) * P],
                           kt_ap(qci)[:, kc * 512:kc * 512 + w],
                           start=True, stop=True,
                           reads=[b_qtb, b_kt], writes=[b_ps1[kc]], tag="sc")
                    dpb = b_ps1[(ext - P) // 512]
                    dve("tensor_tensor", ps1[:, ext - P:ext],
                        ps1[:, ext - P:ext], diagm[:, :], ALU.add,
                        reads=[b_diagm, dpb], writes=[dpb], tag="msk")
                    rmx = cols[:, 8 + cc:9 + cc]
                    dve("reduce_max", rmx, ps1[:, 0:ext], AX.X,
                        reads=[b_ps1[kc] for kc in range(nkc)],
                        writes=[b_cols], tag="rmx")
                    dve("tensor_scalar", rmx, rmx, -SCALE, None, ALU.mult,
                        reads=[b_cols], writes=[b_cols], tag="rm2")
                    act(e_ap(cc)[:, 0:ext], ps1[:, 0:ext], AF.Exp,
                        bias=rmx, scale=SCALE, accum=cols[:, 10 + cc:11 + cc],
                        reads=[b_ps1[kc] for kc in range(nkc)] + [b_cols],
                        writes=[b_e, b_cols], tag="exp")
                if "dbg" in taps and qb == 5 and h == 1:
                    dma("pool", tap_outs["dbg_z"][:, :], cols[:, 8:16],
                        reads=[b_cols], tag="tz")
                dve("reciprocal", cols[:, 12:13], cols[:, 10:11],
                    reads=[b_cols], writes=[b_cols], tag="c1")
                dve("reciprocal", cols[:, 13:14], cols[:, 11:12],
                    reads=[b_cols], writes=[b_cols], tag="c2")
                dve("tensor_tensor", cols[:, 13:14], cols[:, 13:14],
                    smalls[:, SM_NLAM:SM_NLAM + 1], ALU.mult,
                    reads=[b_cols, b_smalls], writes=[b_cols], tag="c2n")
                dve("tensor_scalar", e_ap(0)[:, 0:ext], e_ap(0)[:, 0:ext],
                    cols[:, 12:13], None, ALU.mult,
                    reads=[b_cols], writes=[b_e], tag="an")
                dve("scalar_tensor_tensor", e_ap(0)[:, 0:ext],
                    e_ap(1)[:, 0:ext], cols[:, 13:14], e_ap(0)[:, 0:ext],
                    ALU.mult, ALU.add, reads=[b_cols], writes=[b_e], tag="ac")
                if "dbg" in taps and qb == 5 and h == 1:
                    dma("pool", tap_outs["dbg_e"][:, :], e_ap(0)[:, 0:768],
                        reads=[b_e], tag="te")
                nkb = qb + 1
                for kb0 in range(0, nkb, 8):
                    kb1 = min(kb0 + 8, nkb)
                    for kb in range(kb0, kb1):
                        sl = kb - kb0
                        trans(psT[:, sl * P:(sl + 1) * P],
                              e_ap(0)[:, kb * P:(kb + 1) * P],
                              reads=[b_e], writes=[b_psT], tag="tr")
                    n = (kb1 - kb0) * P
                    act(at_ap[:, kb0 * P:kb0 * P + n], psT[:, 0:n], AF.Copy,
                        reads=[b_psT], writes=[b_at], tag="ate")
                if "dbg" in taps and qb == 5 and h == 1:
                    dma("pool", tap_outs["dbg_at"][:, :], at_ap[:, 0:768],
                        reads=[b_at], tag="ta")
                for ee in range(2):
                    for kb in range(nkb):
                        mm(ps2[:, ee * 512:ee * 512 + P],
                           v_ap(kb)[:, h * 256 + ee * P:h * 256 + (ee + 1) * P],
                           at_ap[:, kb * P:(kb + 1) * P],
                           start=(kb == 0), stop=(kb == nkb - 1),
                           reads=[b_v, b_at], writes=[b_ps2[ee]], tag="av")
                for ee in range(2):
                    act(osq_ap[:, ee * P:(ee + 1) * P],
                        ps2[:, ee * 512:ee * 512 + P], AF.Square,
                        reads=[b_ps2[ee]], writes=[b_osq], tag="osq")
                for ee in range(2):
                    mm(ps3[0:1, 0:P], onescb[:, :],
                       osq_ap[:, ee * P:(ee + 1) * P],
                       start=(ee == 0), stop=(ee == 1),
                       reads=[b_osq, b_ones], writes=[b_ps3], tag="oms")
                act(rows[0:1, 0:P], ps3[0:1, 0:P], AF.Ln,
                    bias=smalls[0:1, SM_EPS:SM_EPS + 1], scale=1.0 / 256.0, reads=[b_ps3], writes=[b_rows],
                    tag="oln")
                act(rows[0:1, 0:P], rows[0:1, 0:P], AF.Exp, scale=-0.5,
                    reads=[b_rows], writes=[b_rows], tag="oex")
                mm(ps3[:, 256:256 + P], onesr[:, :], rows[0:1, 0:P], True,
                   True, reads=[b_rows, b_ones], writes=[b_ps3], tag="obc")
                dve("tensor_copy", rb_ap[:, :], ps3[:, 256:256 + P],
                    reads=[b_ps3], writes=[b_rb], tag="orb")
                for ee in range(2):
                    dve("scalar_tensor_tensor",
                        og_ap(h * 2 + ee)[:, (qb % 4) * P:(qb % 4 + 1) * P],
                        ps2[:, ee * 512:ee * 512 + P],
                        smalls[:, SM_GP + ee:SM_GP + ee + 1], rb_ap[:, :],
                        ALU.mult, ALU.mult,
                        reads=[b_ps2[ee], b_smalls, b_rb], writes=[b_og],
                        tag="on")
            if "dbg" in taps and qb == 7:
                for ok in range(8):
                    dma("pool", tap_outs["dbg_og"][ok * P:(ok + 1) * P, :],
                        og_ap(ok)[:, :], reads=[b_og], tag="to")
            if qb % 4 == 3:
                qbg = qb // 4
                for half in range(2):
                    for ok in range(8):
                        sl2 = ok % 2
                        b_wn = Buf(f"wo{qbg}{half}{ok}")
                        S.claim(b_wn, [wos_slots[sl2]])
                        wos_slots[sl2] = b_wn
                        b_wsg = Buf(f"wsg{qbg}{half}{ok}")
                        S.claim(b_wsg, [b_wostag[sl2]])
                        b_wostag[sl2] = b_wsg
                        dma("sp" if sl2 == 0 else "act",
                            wostag[sl2][:, :],
                            wo_d[ok * P:(ok + 1) * P,
                                 half * 1024:(half + 1) * 1024],
                            writes=[b_wsg], tag="lwo", chan=f"wo{sl2}")
                        dve("tensor_copy",
                            wos_ap[:, sl2 * 1024:sl2 * 1024 + 1024],
                            wostag[sl2][:, :],
                            reads=[b_wsg], writes=[b_wn], tag="wocast")
                        for eo8 in range(8):
                            mm(wop_ps[eo8],
                               wos_ap[:, sl2 * 1024 + eo8 * P:
                                      sl2 * 1024 + (eo8 + 1) * P],
                               og_ap(ok)[:, 0:512],
                               start=(ok == 0), stop=(ok == 7),
                               reads=[b_wn, b_og], writes=[wop_pb[eo8]],
                               tag="wop", commit=True)
                    for eo8 in range(8):
                        eo = half * 8 + eo8
                        sl = eo8 % 2
                        dve("tensor_copy", stg_ap[:, sl * 512:(sl + 1) * 512],
                            wop_ps[eo8],
                            reads=[wop_pb[eo8], wop_pb[7]], writes=[b_stg],
                            tag="wos")
                        if ("parta" in taps and half == 1 and eo8 < 2
                                and qbg in (1, 2)):
                            dsl = (qbg - 1) * 2 + eo8
                            dve("tensor_copy",
                                dbgs[:, dsl * 512:(dsl + 1) * 512],
                                stg_ap[:, sl * 512:(sl + 1) * 512],
                                reads=[b_stg], writes=[b_dbgs], tag="dsn")

                        dma("pool",
                            parta[eo * P:(eo + 1) * P,
                                  qbg * 512:(qbg + 1) * 512],
                            stg_ap[:, sl * 512:(sl + 1) * 512],
                            reads=[b_stg], writes=[b_parta], tag="wod")
        if "parta" in taps:
            tp = tapout("parta", (E, T))
            dma("pool", tp[:, :], parta[:, :], reads=[b_parta], tag="tpa")
            tps = tapout("stgsnap", (P, 2048))
            dma("pool", tps[:, :], dbgs[:, :], reads=[b_dbgs], tag="tsn")
        for k in range(4):
            dyn_dma(k, 2 * E - 512,
                    (lambda off: xch_a[bass.ds(off, 512), :]),
                    parta[k * 512:(k + 1) * 512, :],
                    reads=[b_parta], writes=[b_xch_a], tag="px")
        return [wos_slots[0], wos_slots[1], b_qtb, b_e, b_at, b_og, b_osq,
                b_rb, b_stg, b_kt, b_v, b_ws, b_qst] + prev_xt + ln_scr
    attn_bufs = emit_attn()

    def cc_bar0(e):
        return e.collective_compute(
            "AllGather", ALU.bypass, replica_groups=RGALL,
            ins=[bar_in.ap().opt()], outs=[bar_out[0].ap().opt()])
    S.collective(cc_bar0, reads=[b_xch_a], writes=[b_xch_a], tag="bar0")

    def emit_attn_merge():
        mg_ap = af32(90, 2048)
        b_mg = Buf("mg")
        S.claim(b_mg, [attn_bufs[3]])
        for rk in range(4):
            dyn_dma_in(16, T - TL, partb[rk * TL:(rk + 1) * TL, :],
                       (lambda off, rk=rk:
                        xch_a[rk * TL:(rk + 1) * TL, bass.ds(off, TL)]),
                       reads=[b_xch_a], writes=[b_partb], tag="mgc")
        for f in range(NF):
            for sl in range(2):
                dma("pool", mg_ap[:, sl * TL:(sl + 1) * TL],
                    partb[sl * E + f * P:sl * E + (f + 1) * P, :],
                    reads=[b_partb], writes=[b_mg], tag="mgr")
            dve("tensor_tensor", res_t[:, f * TL:(f + 1) * TL],
                res_t[:, f * TL:(f + 1) * TL], mg_ap[:, 0:TL], ALU.add,
                reads=[b_mg], writes=[b_res], tag="mg1")
            dve("tensor_tensor", res_t[:, f * TL:(f + 1) * TL],
                res_t[:, f * TL:(f + 1) * TL], mg_ap[:, TL:2 * TL], ALU.add,
                reads=[b_mg], writes=[b_res], tag="mg2")
        return [b_mg]
    merge_bufs = emit_attn_merge()
    if "res_a" in taps:
        tp = tapout("res_a", (E, TL))
        for f in range(NF):
            dma("pool", tp[f * P:(f + 1) * P, :], res_t[:, f * TL:(f + 1) * TL],
                reads=[b_res], tag="tra")

    def emit_ffn(idx, prev):
        int_ap = lambda io: abf(0 + io, 512)
        gu_ap = abf(64, 16384)
        xnf_ap = lambda f: abf(96 + f, 512)
        b_int = Buf(f"int{idx}")
        b_gug = Buf(f"gug{idx}")
        b_guu = Buf(f"guu{idx}")
        b_xnf = [Buf(f"fx{idx}_{f}") for f in range(NF)]
        S.claim(b_int, prev)
        S.claim(b_gug, prev)
        S.claim(b_guu, prev)
        for bx in b_xnf:
            S.claim(bx, prev)
        sg_ap = abf(115, 512)
        ug_ap = abf(116, 512)
        b_sg, b_ug = Buf(f"sg{idx}"), Buf(f"ug{idx}")
        S.claim(b_sg, prev)
        S.claim(b_ug, prev)
        scr = prev
        ret_scr = []
        for tch in range(2):
            c0 = tch * 512
            for f in range(NF):
                dve("tensor_copy", xnf_ap(f),
                    res_t[:, f * TL + c0:f * TL + c0 + 512],
                    reads=[b_res], writes=[b_xnf[f]], tag="fc")
            scr = emit_ln_chunk(xnf_ap, b_xnf, f"f{idx}{tch}", 112, scr)
            ret_scr = scr
            for ic in range(16):
                bgn = Buf(f"g{idx}{tch}{ic}")
                S.claim(bgn, [b_gug])
                b_gug = bgn
                bun = Buf(f"u{idx}{tch}{ic}")
                S.claim(bun, [b_guu])
                b_guu = bun
                for gi, base, wb in ((0, 0, b_gug), (1, 256, b_guu)):
                    for r in range(8):
                        dma("sp" if r % 2 == 0 else "act",
                            gu_ap[:, gi * 8192 + r * 1024:
                                  gi * 8192 + (r + 1) * 1024].rearrange(
                                      "p (h i) -> p h i", h=2),
                            wg[768 * r + base:768 * r + base + 256,
                               ic * 512:(ic + 1) * 512].rearrange(
                                   "(h p) i -> p h i", p=P),
                            reads=[b_wg], writes=[wb], tag="gld",
                            chan="gw" if gi == 0 else "uw")
                for io in range(4):
                    iog = ic * 4 + io
                    pg = ps1[:, (io % 2) * 512:(io % 2) * 512 + 512]
                    bg = b_ps1[io % 2]
                    pu = ps1[:, 1024 + (io % 2) * 512:
                             1024 + (io % 2) * 512 + 512]
                    bu = b_ps1[2 + io % 2]
                    for hb in range(16):
                        gcol = (hb // 2) * 1024 + (hb % 2) * 512 + io * P
                        mm(pg, gu_ap[:, gcol:gcol + P], xnf_ap(hb),
                           start=(hb == 0), stop=(hb == 15),
                           reads=[b_gug, b_xnf[hb]], writes=[bg], tag="gmm")
                    for hb in range(16):
                        ucol = 8192 + (hb // 2) * 1024 + (hb % 2) * 512 + io * P
                        mm(pu, gu_ap[:, ucol:ucol + P], xnf_ap(hb),
                           start=(hb == 0), stop=(hb == 15),
                           reads=[b_guu, b_xnf[hb]], writes=[bu], tag="umm")
                    act(sg_ap[:, :], pg, AF.Silu, reads=[bg], writes=[b_sg],
                        tag="sil")
                    dve("tensor_copy", ug_ap[:, :], pu, reads=[bu],
                        writes=[b_ug], tag="ue")
                    dve("tensor_tensor", int_ap(iog)[:, :], sg_ap[:, :],
                        ug_ap[:, :], ALU.mult, reads=[b_sg, b_ug],
                        writes=[b_int], tag="im")
            for ht in range(16):
                r, sub = divmod(ht, 2)
                if ht % 2 == 0:
                    bdw = Buf(f"d{idx}{tch}{ht}")
                    S.claim(bdw, [b_gug])
                    b_gug = bdw
                else:
                    bdw = Buf(f"d{idx}{tch}{ht}")
                    S.claim(bdw, [b_guu])
                    b_guu = bdw
                dma("sp" if ht % 2 == 0 else "act",
                    gu_ap[:, (ht % 2) * 8192:(ht % 2) * 8192 + 8192],
                    wg[768 * r + 512 + sub * P:768 * r + 512 + sub * P + P, :],
                    reads=[b_wg], writes=[bdw], tag="dld",
                    chan=f"dw{ht % 2}")
                pd = ps2[:, (ht % 2) * 512:(ht % 2) * 512 + 512]
                bd = b_ps2[ht % 2]
                for it in range(64):
                    mm(pd, gu_ap[:, (ht % 2) * 8192 + it * P:
                                 (ht % 2) * 8192 + (it + 1) * P],
                       int_ap(it)[:, :], start=(it == 0), stop=(it == 63),
                       reads=[bdw, b_int], writes=[bd], tag="dmm")
                dve("tensor_tensor",
                    res_t[:, ht * TL + c0:ht * TL + c0 + 512],
                    res_t[:, ht * TL + c0:ht * TL + c0 + 512], pd, ALU.add,
                    reads=[bd], writes=[b_res], tag="dac")
        return [b_int, b_gug, b_guu, b_sg, b_ug] + b_xnf + ret_scr

    ffn1_bufs = emit_ffn(0, attn_bufs + merge_bufs)
    if "res_f1" in taps:
        tp = tapout("res_f1", (E, TL))
        for f in range(NF):
            dma("pool", tp[f * P:(f + 1) * P, :], res_t[:, f * TL:(f + 1) * TL],
                reads=[b_res], tag="trf")

    def emit_attnb(prev):
        xh_ap = lambda f: abf(0 + 2 * f, 1024)
        xnb_ap = lambda f: abf(32 + 2 * f, 1024)
        hm_ap = xnb_ap
        PLB = {"ga": 64, "lr": 72, "r": 80, "S": 88, "Phi": 96}
        pla = lambda nm, ct: af32(PLB[nm] + ct, 256)
        gx_ap = lambda ct: abf(104, 4096)[:, ct * 256:(ct + 1) * 256]
        tmp = lambda i: abf(108, 4096)[:, i * 256:(i + 1) * 256]
        wsl_ap = abf(64, 16384)
        wgz_ap = abf(112, 4096)
        b_xh = [Buf(f"xh{f}") for f in range(NF)]
        b_xnb = [Buf(f"bx{f}") for f in range(NF)]
        b_hm = [Buf(f"hm{f}") for f in range(NF)]
        b_pl, b_gx, b_tmp, b_wgz = Buf("pl"), Buf("gx"), Buf("tmp"), Buf("wgz")
        b_wsl = Buf("wsl")
        for bx in b_xh + b_xnb + [b_gx, b_wgz, b_wsl]:
            S.claim(bx, prev)
        scr = prev
        for tch in range(2):
            c0 = tch * 512
            for f in range(NF):
                dve("tensor_copy", xnb_ap(f)[:, c0:c0 + 512],
                    res_t[:, f * TL + c0:f * TL + c0 + 512],
                    reads=[b_res], writes=[b_xnb[f]], tag="bc")
            scr = emit_ln_chunk(
                (lambda cc0: (lambda f: xnb_ap(f)[:, cc0:cc0 + 512]))(c0),
                b_xnb, f"b{tch}", 108, scr)
        S.claim(b_tmp, scr + [b_pl])
        for dh in range(2):
            b_wn = Buf(f"wi{dh}")
            S.claim(b_wn, [b_wsl])
            b_wsl = b_wn
            for ek in range(16):
                dma("pool",
                    wsl_ap[:, ek * 1024:(ek + 1) * 1024],
                    win_d[ek * P:(ek + 1) * P, dh * 1024:(dh + 1) * 1024],
                    writes=[b_wsl], tag="wld", chan="wi")
            for do in range(8):
                dof = dh * 8 + do
                for tch in range(2):
                    pslice = ps1[:, (tch % 2) * 512:(tch % 2) * 512 + 512]
                    pbuf = b_ps1[tch % 2]
                    for ek in range(16):
                        mm(pslice, wsl_ap[:, ek * 1024 + do * P:
                                          ek * 1024 + (do + 1) * P],
                           xnb_ap(ek)[:, tch * 512:tch * 512 + 512],
                           start=(ek == 0), stop=(ek == 15),
                           reads=[b_wsl, b_xnb[ek]], writes=[pbuf], tag="xm")
                    dve("tensor_scalar",
                        xh_ap(dof)[:, tch * 512:tch * 512 + 512], pslice,
                        smalls[:, SM_BIN + dof:SM_BIN + dof + 1], None,
                        ALU.add,
                        reads=[pbuf, b_smalls], writes=[b_xh[dof]], tag="xe")
        if "xh" in taps:
            tp = tapout("xh", (E, TL), BF16)
            for f in range(NF):
                dma("pool", tp[f * P:(f + 1) * P, :], xh_ap(f),
                    reads=[b_xh[f]], tag="txh")
        for wd, off in ((wgx_d, 0), (wga_d, 2048)):
            for i in range(16):
                dma("pool", wgz_ap[:, off + i * P:off + (i + 1) * P],
                    wd[i * P:(i + 1) * P, :], writes=[b_wgz], tag="wgz",
                    chan="wgz")
        for f in range(NF):
            S.claim(b_hm[f], [b_xnb[f], b_wsl])
        S.claim(b_pl, [b_wsl] + list(prev))
        for ck in range(4):
            c0 = ck * 256
            for blk in range(8):
                for k2 in range(2):
                    mm(ps3[:, 0:256], wgz_ap[:, 2048 + (blk * 2 + k2) * P:
                                             2048 + (blk * 2 + k2 + 1) * P],
                       xh_ap(blk * 2 + k2)[:, c0:c0 + 256],
                       start=(k2 == 0), stop=(k2 == 1),
                       reads=[b_wgz, b_xh[blk * 2 + k2]], writes=[b_ps3],
                       tag="gam")
                act(pla("ga", blk)[:, :], ps3[:, 0:256], AF.Sigmoid,
                    bias=smalls[:, SM_BGA + blk:SM_BGA + blk + 1],
                    reads=[b_ps3, b_smalls], writes=[b_pl], tag="gae")
                for k2 in range(2):
                    mm(ps3[:, 256:512], wgz_ap[:, (blk * 2 + k2) * P:
                                               (blk * 2 + k2 + 1) * P],
                       xh_ap(blk * 2 + k2)[:, c0:c0 + 256],
                       start=(k2 == 0), stop=(k2 == 1),
                       reads=[b_wgz, b_xh[blk * 2 + k2]], writes=[b_ps3],
                       tag="gxm")
                act(gx_ap(blk)[:, :], ps3[:, 256:512], AF.Sigmoid,
                    bias=smalls[:, SM_BGX + blk:SM_BGX + blk + 1],
                    reads=[b_ps3, b_smalls], writes=[b_gx], tag="gxe")
            for ct in range(8):
                A = pla("ga", ct)
                BB = pla("lr", ct)
                TH = pla("S", ct)
                PH = pla("Phi", ct)
                dve("tensor_scalar", BB[:, :], A[:, :],
                    smalls[:, SM_N8SP + ct:SM_N8SP + ct + 1], None, ALU.mult,
                    reads=[b_pl, b_smalls], writes=[b_pl], tag="lr")
                dve("tensor_scalar", TH[:, :], A[:, :],
                    smalls[:, SM_AI + ct:SM_AI + ct + 1], None, ALU.mult,
                    reads=[b_pl, b_smalls], writes=[b_pl], tag="th")
                dve("tensor_tensor_scan", PH[:, :], ones256[:, :], TH[:, :],
                    carry[:, ct:ct + 1], ALU.mult, ALU.add,
                    reads=[b_pl, b_carry, b_ones], writes=[b_pl], tag="cph")
                dve("tensor_copy", carry[:, ct:ct + 1], PH[:, 255:256],
                    reads=[b_pl], writes=[b_carry], tag="cphc")
                dve("tensor_scalar", A[:, :], TH[:, :], 1.0 / math.pi, MAGIC,
                    ALU.mult, ALU.add, reads=[b_pl], writes=[b_pl], tag="q1")
                dve("tensor_scalar", A[:, :], A[:, :], MAGIC, None,
                    ALU.subtract, reads=[b_pl], writes=[b_pl], tag="q2")
                dve("scalar_tensor_tensor", A[:, :], A[:, :], -math.pi,
                    TH[:, :], ALU.mult, ALU.add, reads=[b_pl], writes=[b_pl],
                    tag="q3")
                act(tmp(1)[:, :], A[:, :], AF.Sin, scale=2.0,
                    reads=[b_pl], writes=[b_tmp], tag="s2t")
                dve("tensor_scalar", A[:, :], TH[:, :], 1.0 / math.pi,
                    MAGIC + 0.25, ALU.mult, ALU.add, reads=[b_pl],
                    writes=[b_pl], tag="q4")
                dve("tensor_scalar", A[:, :], A[:, :], MAGIC, None,
                    ALU.subtract, reads=[b_pl], writes=[b_pl], tag="q5")
                dve("scalar_tensor_tensor", A[:, :], A[:, :], -math.pi,
                    TH[:, :], ALU.mult, ALU.add, reads=[b_pl], writes=[b_pl],
                    tag="q6")
                act(tmp(0)[:, :], A[:, :], AF.Sin, scale=2.0,
                    bias=smalls[:, SM_PIH:SM_PIH + 1],
                    reads=[b_pl], writes=[b_tmp], tag="c2t")
                dve("tensor_scalar", A[:, :], PH[:, :], 1.0 / TWO_PI, MAGIC,
                    ALU.mult, ALU.add, reads=[b_pl], writes=[b_pl], tag="q7")
                dve("tensor_scalar", A[:, :], A[:, :], MAGIC, None,
                    ALU.subtract, reads=[b_pl], writes=[b_pl], tag="q8")
                dve("scalar_tensor_tensor", A[:, :], A[:, :], -TWO_PI,
                    PH[:, :], ALU.mult, ALU.add, reads=[b_pl], writes=[b_pl],
                    tag="q9")
                act(tmp(6)[:, :], A[:, :], AF.Sin,
                    reads=[b_pl], writes=[b_tmp], tag="snp")
                dve("tensor_scalar", A[:, :], PH[:, :], 1.0 / TWO_PI,
                    MAGIC + 0.25, ALU.mult, ALU.add, reads=[b_pl],
                    writes=[b_pl], tag="qa")
                dve("tensor_scalar", A[:, :], A[:, :], MAGIC, None,
                    ALU.subtract, reads=[b_pl], writes=[b_pl], tag="qb")
                dve("scalar_tensor_tensor", A[:, :], A[:, :], -TWO_PI,
                    PH[:, :], ALU.mult, ALU.add, reads=[b_pl], writes=[b_pl],
                    tag="qc")
                act(tmp(7)[:, :], A[:, :], AF.Sin,
                    bias=smalls[:, SM_PIH:SM_PIH + 1],
                    reads=[b_pl], writes=[b_tmp], tag="csp")
                dve("tensor_tensor_scan", TH[:, :], ones256[:, :], BB[:, :],
                    carry[:, 8 + ct:9 + ct], ALU.mult, ALU.add,
                    reads=[b_pl, b_carry, b_ones], writes=[b_pl], tag="cs")
                dve("tensor_copy", carry[:, 8 + ct:9 + ct], TH[:, 255:256],
                    reads=[b_pl], writes=[b_carry], tag="csc")
                act(pla("r", ct)[:, :], BB[:, :], AF.Exp,
                    reads=[b_pl], writes=[b_pl], tag="re")
                act(A[:, :], BB[:, :], AF.Exp, scale=2.0,
                    reads=[b_pl], writes=[b_pl], tag="aa")
                dve("tensor_tensor", BB[:, :], A[:, :], tmp(0)[:, :],
                    ALU.mult, reads=[b_pl, b_tmp], writes=[b_pl], tag="z1")
                dve("tensor_scalar", BB[:, :], BB[:, :], -1.0, 1.0, ALU.mult,
                    ALU.add, reads=[b_pl], writes=[b_pl], tag="z2")
                dve("tensor_tensor", tmp(2)[:, :], A[:, :], tmp(1)[:, :],
                    ALU.mult, reads=[b_pl, b_tmp], writes=[b_tmp], tag="z3")
                act(A[:, :], BB[:, :], AF.Square, reads=[b_pl], writes=[b_pl],
                    tag="m1")
                act(tmp(3)[:, :], tmp(2)[:, :], AF.Square,
                    reads=[b_tmp], writes=[b_tmp], tag="m2")
                dve("tensor_tensor", A[:, :], A[:, :], tmp(3)[:, :], ALU.add,
                    reads=[b_pl, b_tmp], writes=[b_pl], tag="m3")
                act(A[:, :], A[:, :], AF.Ln,
                    bias=smalls[:, SM_TINY:SM_TINY + 1],
                    reads=[b_pl], writes=[b_pl], tag="m4")
                act(A[:, :], A[:, :], AF.Exp, scale=0.5,
                    reads=[b_pl], writes=[b_pl], tag="m5")
                dve("tensor_tensor", tmp(4)[:, :], A[:, :], BB[:, :], ALU.add,
                    reads=[b_pl], writes=[b_tmp], tag="sr1")
                dve("tensor_scalar_max", tmp(4)[:, :], tmp(4)[:, :], 0.0,
                    reads=[b_tmp], writes=[b_tmp], tag="sr2")
                act(tmp(4)[:, :], tmp(4)[:, :], AF.Ln,
                    bias=smalls[:, SM_TINY:SM_TINY + 1], scale=0.5,
                    reads=[b_tmp], writes=[b_tmp], tag="sr3")
                act(tmp(4)[:, :], tmp(4)[:, :], AF.Exp, scale=0.5,
                    reads=[b_tmp], writes=[b_tmp], tag="sr4")
                dve("tensor_tensor", tmp(5)[:, :], A[:, :], BB[:, :],
                    ALU.subtract, reads=[b_pl], writes=[b_tmp], tag="si1")
                dve("tensor_scalar_max", tmp(5)[:, :], tmp(5)[:, :], 0.0,
                    reads=[b_tmp], writes=[b_tmp], tag="si2")
                act(tmp(5)[:, :], tmp(5)[:, :], AF.Ln,
                    bias=smalls[:, SM_TINY:SM_TINY + 1], scale=0.5,
                    reads=[b_tmp], writes=[b_tmp], tag="si3")
                act(tmp(5)[:, :], tmp(5)[:, :], AF.Exp, scale=0.5,
                    reads=[b_tmp], writes=[b_tmp], tag="si4")
                act(tmp(3)[:, :], tmp(2)[:, :], AF.Sign,
                    reads=[b_tmp], writes=[b_tmp], tag="sg1")
                dve("tensor_scalar", tmp(3)[:, :], tmp(3)[:, :], -1.0, None,
                    ALU.mult, reads=[b_tmp], writes=[b_tmp], tag="sg2")
                dve("tensor_tensor", tmp(5)[:, :], tmp(5)[:, :], tmp(3)[:, :],
                    ALU.mult, reads=[b_tmp], writes=[b_tmp], tag="sg3")
                dve("tensor_tensor", tmp(2)[:, :], gx_ap(ct)[:, :],
                    xh_ap(ct)[:, c0:c0 + 256], ALU.mult,
                    reads=[b_gx, b_xh[ct]], writes=[b_tmp], tag="xp1")
                dve("tensor_tensor", tmp(3)[:, :], gx_ap(ct)[:, :],
                    xh_ap(8 + ct)[:, c0:c0 + 256], ALU.mult,
                    reads=[b_gx, b_xh[8 + ct]], writes=[b_tmp], tag="xp2")
                dve("tensor_tensor", tmp(1)[:, :], tmp(4)[:, :], tmp(2)[:, :],
                    ALU.mult, reads=[b_tmp], writes=[b_tmp], tag="c1")
                dve("tensor_tensor", tmp(0)[:, :], tmp(5)[:, :], tmp(3)[:, :],
                    ALU.mult, reads=[b_tmp], writes=[b_tmp], tag="c2")
                dve("tensor_tensor", tmp(1)[:, :], tmp(1)[:, :], tmp(0)[:, :],
                    ALU.subtract, reads=[b_tmp], writes=[b_tmp], tag="c3")
                dve("tensor_tensor", tmp(0)[:, :], tmp(4)[:, :], tmp(3)[:, :],
                    ALU.mult, reads=[b_tmp], writes=[b_tmp], tag="c4")
                dve("tensor_tensor", tmp(2)[:, :], tmp(5)[:, :], tmp(2)[:, :],
                    ALU.mult, reads=[b_tmp], writes=[b_tmp], tag="c5")
                dve("tensor_tensor", tmp(0)[:, :], tmp(0)[:, :], tmp(2)[:, :],
                    ALU.add, reads=[b_tmp], writes=[b_tmp], tag="c6")
                dve("tensor_tensor", tmp(2)[:, :], tmp(7)[:, :], tmp(1)[:, :],
                    ALU.mult, reads=[b_tmp], writes=[b_tmp], tag="u1")
                dve("tensor_tensor", tmp(3)[:, :], tmp(6)[:, :], tmp(0)[:, :],
                    ALU.mult, reads=[b_tmp], writes=[b_tmp], tag="u2")
                dve("tensor_tensor", tmp(2)[:, :], tmp(2)[:, :], tmp(3)[:, :],
                    ALU.add, reads=[b_tmp], writes=[b_tmp], tag="u3")
                dve("tensor_tensor", tmp(3)[:, :], tmp(7)[:, :], tmp(0)[:, :],
                    ALU.mult, reads=[b_tmp], writes=[b_tmp], tag="u4")
                dve("tensor_tensor", tmp(4)[:, :], tmp(6)[:, :], tmp(1)[:, :],
                    ALU.mult, reads=[b_tmp], writes=[b_tmp], tag="u5")
                dve("tensor_tensor", tmp(3)[:, :], tmp(3)[:, :], tmp(4)[:, :],
                    ALU.subtract, reads=[b_tmp], writes=[b_tmp], tag="u6")
                dve("tensor_tensor_scan", BB[:, :], pla("r", ct)[:, :],
                    tmp(2)[:, :], carry[:, 16 + ct:17 + ct], ALU.mult, ALU.add,
                    reads=[b_pl, b_tmp, b_carry], writes=[b_pl], tag="wr")
                dve("tensor_copy", carry[:, 16 + ct:17 + ct], BB[:, 255:256],
                    reads=[b_pl], writes=[b_carry], tag="wrc")
                dve("tensor_tensor_scan", A[:, :], pla("r", ct)[:, :],
                    tmp(3)[:, :], carry[:, 24 + ct:25 + ct], ALU.mult, ALU.add,
                    reads=[b_pl, b_tmp, b_carry], writes=[b_pl], tag="wi")
                dve("tensor_copy", carry[:, 24 + ct:25 + ct], A[:, 255:256],
                    reads=[b_pl], writes=[b_carry], tag="wic")
                dve("tensor_tensor", tmp(4)[:, :], tmp(7)[:, :], BB[:, :],
                    ALU.mult, reads=[b_pl, b_tmp], writes=[b_tmp], tag="h1")
                dve("tensor_tensor", tmp(5)[:, :], tmp(6)[:, :], A[:, :],
                    ALU.mult, reads=[b_pl, b_tmp], writes=[b_tmp], tag="h2")
                dve("tensor_tensor", hm_ap(ct)[:, c0:c0 + 256], tmp(4)[:, :],
                    tmp(5)[:, :], ALU.subtract,
                    reads=[b_tmp], writes=[b_hm[ct]], tag="h3")
                dve("tensor_tensor", tmp(4)[:, :], tmp(6)[:, :], BB[:, :],
                    ALU.mult, reads=[b_pl, b_tmp], writes=[b_tmp], tag="h4")
                dve("tensor_tensor", tmp(5)[:, :], tmp(7)[:, :], A[:, :],
                    ALU.mult, reads=[b_pl, b_tmp], writes=[b_tmp], tag="h5")
                dve("tensor_tensor", hm_ap(8 + ct)[:, c0:c0 + 256],
                    tmp(4)[:, :], tmp(5)[:, :], ALU.add,
                    reads=[b_tmp], writes=[b_hm[8 + ct]], tag="h6")
                act(tmp(4)[:, :], TH[:, :], AF.Exp,
                    reads=[b_pl], writes=[b_tmp], tag="pse")
                dve("tensor_tensor", tmp(5)[:, :], tmp(4)[:, :], tmp(7)[:, :],
                    ALU.mult, reads=[b_tmp], writes=[b_tmp], tag="pr")
                dma("sp", pdram[ct * P:(ct + 1) * P, c0:c0 + 256],
                    tmp(5)[:, :], reads=[b_tmp], writes=[b_pdram], tag="pd1",
                    chan="pds")
                dve("tensor_tensor", tmp(5)[:, :], tmp(4)[:, :], tmp(6)[:, :],
                    ALU.mult, reads=[b_tmp], writes=[b_tmp], tag="pi")
                dma("sp", pdram[TL + ct * P:TL + (ct + 1) * P, c0:c0 + 256],
                    tmp(5)[:, :], reads=[b_tmp], writes=[b_pdram], tag="pd2",
                    chan="pds")
        for ct in range(8):
            dve("tensor_copy", hend[:, 2 * ct:2 * ct + 1],
                hm_ap(ct)[:, TL - 1:TL], reads=[b_hm[ct]], writes=[b_hend],
                tag="he1")
            dve("tensor_copy", hend[:, 2 * ct + 1:2 * ct + 2],
                hm_ap(8 + ct)[:, TL - 1:TL], reads=[b_hm[8 + ct]],
                writes=[b_hend], tag="he2")
        for ct in range(8):
            dma("pool", partc[ct * P:(ct + 1) * P, :],
                hend[:, 2 * ct:2 * ct + 2],
                reads=[b_hend], writes=[b_partc], tag="hw")
        dyn_dma(17, 2 * TL - TL,
                (lambda off: xch_b[bass.ds(off, TL), :]), partc[:, :],
                reads=[b_partc], writes=[b_xch_b], tag="hwc")

        def cc_bar1(e):
            return e.collective_compute(
                "AllGather", ALU.bypass, replica_groups=RGALL,
                ins=[bar_in.ap().opt()], outs=[bar_out[1].ap().opt()])
        S.collective(cc_bar1, reads=[b_xch_b], writes=[b_xch_b], tag="bar1")
        dyn_dma_in(25, 2 * TL - TL, partd[:, :],
                   (lambda off: xch_b[bass.ds(off, TL), :]),
                   reads=[b_xch_b], writes=[b_partd], tag="hrc")
        for ct in range(8):
            dma("pool", hin[:, 2 * ct:2 * ct + 2],
                partd[ct * P:(ct + 1) * P, :],
                reads=[b_partd], writes=[b_hin], tag="hr")
        dve("tensor_scalar", hin[:, :], hin[:, :],
            smalls[:, SM_HM:SM_HM + 1], None, ALU.mult,
            reads=[b_smalls], writes=[b_hin], tag="hmm")
        for ck in range(4):
            c0 = ck * 256
            for ct in range(8):
                dma("pool", tmp(0)[:, :],
                    pdram[ct * P:(ct + 1) * P, c0:c0 + 256],
                    reads=[b_pdram], writes=[b_tmp], tag="fp1")
                dma("pool", tmp(1)[:, :],
                    pdram[TL + ct * P:TL + (ct + 1) * P, c0:c0 + 256],
                    reads=[b_pdram], writes=[b_tmp], tag="fp2")
                dve("tensor_scalar", tmp(2)[:, :], tmp(0)[:, :],
                    hin[:, 2 * ct:2 * ct + 1], None, ALU.mult,
                    reads=[b_tmp, b_hin], writes=[b_tmp], tag="f1")
                dve("tensor_scalar", tmp(3)[:, :], tmp(1)[:, :],
                    hin[:, 2 * ct + 1:2 * ct + 2], None, ALU.mult,
                    reads=[b_tmp, b_hin], writes=[b_tmp], tag="f2")
                dve("tensor_tensor", tmp(2)[:, :], tmp(2)[:, :], tmp(3)[:, :],
                    ALU.subtract, reads=[b_tmp], writes=[b_tmp], tag="f3")
                dve("tensor_tensor", hm_ap(ct)[:, c0:c0 + 256],
                    hm_ap(ct)[:, c0:c0 + 256], tmp(2)[:, :], ALU.add,
                    reads=[b_tmp], writes=[b_hm[ct]], tag="f4")
                dve("tensor_scalar", tmp(2)[:, :], tmp(1)[:, :],
                    hin[:, 2 * ct:2 * ct + 1], None, ALU.mult,
                    reads=[b_tmp, b_hin], writes=[b_tmp], tag="f5")
                dve("tensor_scalar", tmp(3)[:, :], tmp(0)[:, :],
                    hin[:, 2 * ct + 1:2 * ct + 2], None, ALU.mult,
                    reads=[b_tmp, b_hin], writes=[b_tmp], tag="f6")
                dve("tensor_tensor", tmp(2)[:, :], tmp(2)[:, :], tmp(3)[:, :],
                    ALU.add, reads=[b_tmp], writes=[b_tmp], tag="f7")
                dve("tensor_tensor", hm_ap(8 + ct)[:, c0:c0 + 256],
                    hm_ap(8 + ct)[:, c0:c0 + 256], tmp(2)[:, :], ALU.add,
                    reads=[b_tmp], writes=[b_hm[8 + ct]], tag="f8")
        b_wsl2 = Buf("wo2")
        S.claim(b_wsl2, [b_pl])
        for dh in range(2):
            if dh == 1:
                bw = Buf("wo2b")
                S.claim(bw, [b_wsl2])
                b_wsl2 = bw
            for ek in range(16):
                dma("pool",
                    wsl_ap[:, ek * 1024:(ek + 1) * 1024],
                    wout_d[ek * P:(ek + 1) * P, dh * 1024:(dh + 1) * 1024],
                    writes=[b_wsl2], tag="wod")
            for do in range(8):
                dof = dh * 8 + do
                for tch in range(2):
                    pslice = ps1[:, (tch % 2) * 512:(tch % 2) * 512 + 512]
                    pbuf = b_ps1[tch % 2]
                    for ek in range(16):
                        mm(pslice, wsl_ap[:, ek * 1024 + do * P:
                                          ek * 1024 + (do + 1) * P],
                           hm_ap(ek)[:, tch * 512:tch * 512 + 512],
                           start=(ek == 0), stop=(ek == 15),
                           reads=[b_wsl2, b_hm[ek]], writes=[pbuf], tag="om")
                    dve("scalar_tensor_tensor",
                        res_t[:, dof * TL + tch * 512:
                              dof * TL + tch * 512 + 512],
                        pslice, smalls[:, SM_BOUT + dof:SM_BOUT + dof + 1],
                        res_t[:, dof * TL + tch * 512:
                              dof * TL + tch * 512 + 512],
                        ALU.add, ALU.add,
                        reads=[pbuf, b_smalls], writes=[b_res], tag="oe")
        return b_xh + b_hm + [b_pl, b_gx, b_tmp, b_wgz, b_wsl2]

    attnb_bufs = emit_attnb(ffn1_bufs)
    if "res_b" in taps:
        tp = tapout("res_b", (E, TL))
        for f in range(NF):
            dma("pool", tp[f * P:(f + 1) * P, :], res_t[:, f * TL:(f + 1) * TL],
                reads=[b_res], tag="trb")

    emit_ffn(1, attnb_bufs)

    b_outd = Buf("outd")
    for f in range(NF):
        dma("sp", out_res[f * P:(f + 1) * P, :], res_t[:, f * TL:(f + 1) * TL],
            reads=[b_res], writes=[b_outd], tag="out", chan="out")
    S.op("sp", lambda e: e.nop(), reads=[b_outd], tag="fin")

    blk = es.enter_context(nc.Block())
    S.build(nc, es, blk)
    es.close()
    return nc, list(tap_outs.keys())



def _host_inputs(inputs, core, consts):
    b, par = core // 2, core % 2
    CmatT, u_lhs, b_lhs, a_lhs = consts
    x = np.asarray(inputs["L0_attn_a_x"])
    d = {}
    d["x_lo"] = np.ascontiguousarray(x[b, 0:TL, :].T)
    d["x_hi"] = np.ascontiguousarray(x[b, TL:, :].T)
    d["x_res"] = d["x_lo"] if par == 0 else d["x_hi"]
    cs = par * 1024
    d["wq"] = np.ascontiguousarray(np.asarray(inputs["L0_attn_a_Wq"])[:, cs:cs + 1024])
    d["wk"] = np.ascontiguousarray(np.asarray(inputs["L0_attn_a_Wk"])[:, cs:cs + 1024])
    d["wv"] = np.ascontiguousarray(np.asarray(inputs["L0_attn_a_Wv"])[:, cs:cs + 1024])
    d["wo"] = np.ascontiguousarray(np.asarray(inputs["L0_attn_a_Wo"])[cs:cs + 1024, :])
    d["win"] = np.ascontiguousarray(np.asarray(inputs["L0_attn_b_W_in"]))
    d["wout"] = np.ascontiguousarray(np.asarray(inputs["L0_attn_b_W_out"]))
    d["wgx"] = np.ascontiguousarray(
        np.asarray(inputs["L0_attn_b_W_gx"]).reshape(16 * P, P))
    d["wga"] = np.ascontiguousarray(
        np.asarray(inputs["L0_attn_b_W_ga"]).reshape(16 * P, P))
    sm = np.zeros((P, 96), np.float32)
    sm[:, SM_LAM + 0] = np.asarray(inputs["L0_attn_a_lambda_q1"])
    sm[:, SM_LAM + 1] = np.asarray(inputs["L0_attn_a_lambda_k1"])
    sm[:, SM_LAM + 2] = np.asarray(inputs["L0_attn_a_lambda_q2"])
    sm[:, SM_LAM + 3] = np.asarray(inputs["L0_attn_a_lambda_k2"])
    g = np.asarray(inputs["L0_attn_a_g"])
    sm[:, SM_G + 0] = g[0:P]
    sm[:, SM_G + 1] = g[P:2 * P]
    sm[:, SM_BIN:SM_BIN + NF] = np.asarray(inputs["L0_attn_b_b_in"]).reshape(NF, P).T
    sm[:, SM_BOUT:SM_BOUT + NF] = np.asarray(inputs["L0_attn_b_b_out"]).reshape(NF, P).T
    sm[:, SM_BGX:SM_BGX + 8] = np.asarray(inputs["L0_attn_b_b_gx"]).T
    sm[:, SM_BGA:SM_BGA + 8] = np.asarray(inputs["L0_attn_b_b_ga"]).T
    sm[:, SM_AR:SM_AR + 8] = np.asarray(inputs["L0_attn_b_a_real"]).reshape(8, P).T
    sm[:, SM_AI:SM_AI + 8] = np.asarray(inputs["L0_attn_b_a_imag"]).reshape(8, P).T
    sm[:, SM_HM] = float(par)
    sm[:, SM_EPS] = EPS
    sm[:, SM_NEG1] = -1.0
    sm[:, SM_PIH] = PIH
    sm[:, SM_TINY] = 1e-38
    d["smalls"] = sm
    offv = np.zeros(40, np.int32)
    for k in range(4):
        offv[k] = par * E + k * 512
    offv[16] = par * TL
    offv[17] = par * TL
    offv[25] = (1 - par) * TL
    d["offs"] = offv[None, :]
    dm = np.zeros((P, P), np.float32)
    dm[np.triu_indices(P, 1)] = NEG
    d["diagm"] = dm
    d["ident"] = np.eye(P, dtype=np.float32)
    d["cmatT"], d["u_lhs"], d["b_lhs"], d["a_lhs"] = CmatT, u_lhs, b_lhs, a_lhs
    gu = _spline_aux("gu", core, [("g", np.asarray(inputs["L0_ffn_gate_cp"])),
                                  ("u", np.asarray(inputs["L0_ffn_up_cp"]))])
    dn = _spline_aux("d", core, [("d", np.asarray(inputs["L0_ffn_down_cp"]))])
    d["gu_phim"], d["gu_u2"] = gu["phim"], gu["u2"]
    d["dn_phim"], d["dn_u2"] = dn["phim"], dn["u2"]
    d["cpw20_g"], d["cpw5_g"] = gu["cpw20g"], gu["cpw5g"]
    d["cpw20_u"], d["cpw5_u"] = gu["cpw20u"], gu["cpw5u"]
    d["cpw20_d"], d["cpw5_d"] = dn["cpw20d"], dn["cpw5d"]
    return d


_CACHE = {}


def kernel(**inputs):
    if "prog" not in _CACHE:
        _CACHE["prog"] = build_program()
    nc, _ = _CACHE["prog"]
    consts = _spline_const_tables()
    in_maps = [_host_inputs(inputs, c, consts) for c in range(8)]
    res = run_bass_kernel_spmd(nc, in_maps, core_ids=list(range(8)))
    out = np.empty((B, T, E), np.float32)
    for c in range(8):
        b, par = c // 2, c % 2
        out[b, par * TL:(par + 1) * TL, :] = res.results[c]["out_res"].T
    return out
